# version 02
# baseline (speedup 1.0000x reference)
import contextlib
import numpy as np
import ml_dtypes
import concourse.bass as bass
import concourse.mybir as mybir
from concourse.bass_utils import run_bass_kernel_spmd

F32 = mybir.dt.float32
BF16 = mybir.dt.bfloat16
AF = mybir.ActivationFunctionType
ALU = mybir.AluOpType
AX = mybir.AxisListType

D = 2048
TB = 2048
OWN = 1024
HFF = 5632
NW = float(HFF * D)
EA = [0, 1, 2, 3, 8, 9, 10, 11]
EB = list(range(16))
MASKED_B = {4, 5, 6, 7, 12, 13, 14, 15}


class Sem:
    def __init__(self, h):
        self.h = h
        self.n = 0


class Build:
    ENG = ("tensor", "vector", "scalar", "gpsimd", "sync")

    def __init__(self):
        self.q = {e: [] for e in self.ENG}
        self.auto = {}

    def emit(self, eng, fn):
        self.q[eng].append(fn)

    def wait(self, eng, sem, val):
        if val > 0:
            self.emit(eng, lambda e, _s=sem.h, _v=val: e.wait_ge(_s, _v))

    def dma(self, qeng, sem, dst, src, transpose=False):
        sem.n += 16
        tgt = sem.n

        def f(e, _d=dst, _s=src, _t=transpose, _h=sem.h):
            if _t:
                e.dma_start_transpose(_d, _s).then_inc(_h, 16)
            else:
                e.dma_start(_d, _s).then_inc(_h, 16)

        self.emit(qeng, f)
        return tgt

    def op(self, eng, fn, sem=None, k=1):
        a = self.auto.get(eng)
        if a is not None:
            self.wait(eng, a, a.n)
            if sem is None:
                sem = a
        if sem is None:
            self.emit(eng, lambda e, _f=fn: _f(e))
            return None
        sem.n += k
        tgt = sem.n

        def f(e, _f=fn, _h=sem.h, _k=k):
            _f(e).then_inc(_h, _k)

        self.emit(eng, f)
        return tgt


def build_program():
    nc = bass.Bass("TRN2", target_bir_lowering=False)

    X = nc.dram_tensor("X", [TB, D], F32, kind="ExternalInput").ap()
    Wq = nc.dram_tensor("Wq", [D, 4096], F32, kind="ExternalInput").ap()
    Wk = nc.dram_tensor("Wk", [D, D], F32, kind="ExternalInput").ap()
    Wv = nc.dram_tensor("Wv", [D, D], F32, kind="ExternalInput").ap()
    Wo = nc.dram_tensor("Wo", [4096, D], F32, kind="ExternalInput").ap()
    W1 = nc.dram_tensor("W1", [HFF, D], F32, kind="ExternalInput").ap()
    W2 = nc.dram_tensor("W2", [HFF, D], F32, kind="ExternalInput").ap()
    W3 = nc.dram_tensor("W3", [D, HFF], F32, kind="ExternalInput").ap()
    COSB = nc.dram_tensor("COSB", [256, TB], F32, kind="ExternalInput").ap()
    SINB = nc.dram_tensor("SINB", [256, TB], F32, kind="ExternalInput").ap()
    MSK = nc.dram_tensor("MSK", [16 * 128, 512], BF16, kind="ExternalInput").ap()
    Y = nc.dram_tensor("Y", [OWN, D], F32, kind="ExternalOutput").ap()

    xn1s = nc.dram_tensor("xn1s", [TB, D], BF16, kind="Internal").ap()
    Ks = nc.dram_tensor("Ks", [D, TB], BF16, kind="Internal").ap()
    Vs2 = nc.dram_tensor("Vs2", [4 * TB, 512], BF16, kind="Internal").ap()
    x2s = nc.dram_tensor("x2s", [OWN, D], F32, kind="Internal").ap()
    xn2s = nc.dram_tensor("xn2s", [OWN, D], BF16, kind="Internal").ap()
    w1qn = nc.dram_tensor("w1qn", [HFF, D], BF16, kind="Internal").ap()
    w2qn = nc.dram_tensor("w2qn", [HFF, D], BF16, kind="Internal").ap()
    w3qn = nc.dram_tensor("w3qn", [D, HFF], BF16, kind="Internal").ap()
    w1qt = nc.dram_tensor("w1qt", [D, HFF], BF16, kind="Internal").ap()
    w2qt = nc.dram_tensor("w2qt", [D, HFF], BF16, kind="Internal").ap()

    def r3(t, p=128):
        return t.rearrange("(c p) f -> p c f", p=p)

    with contextlib.ExitStack() as stk:
        E = stk.enter_context
        R1 = E(nc.sbuf_tensor("R1", [128, 32768], BF16))
        R2 = E(nc.sbuf_tensor("R2", [128, 16384], BF16))
        R3 = E(nc.sbuf_tensor("R3", [128, 16384], BF16))
        R4 = E(nc.sbuf_tensor("R4", [128, 16384], BF16))
        qp = E(nc.sbuf_tensor("qp", [128, 4096], BF16))
        pbuf = E(nc.sbuf_tensor("pbuf", [128, 2048], BF16))
        tmp = E(nc.sbuf_tensor("tmp", [128, 2048], F32))
        bcs = E(nc.sbuf_tensor("bcs", [128, 1024], F32))
        onec = E(nc.sbuf_tensor("onec", [128, 1], BF16))
        onecf = E(nc.sbuf_tensor("onecf", [128, 1], F32))
        onerf = E(nc.sbuf_tensor("onerf", [1, 128], F32))
        sml = E(nc.sbuf_tensor("sml", [128, 16], F32))
        ln2m = E(nc.sbuf_tensor("ln2m", [128, 12], F32))
        gpart = E(nc.sbuf_tensor("gpart", [128, 192], F32))
        gsml = E(nc.sbuf_tensor("gsml", [1, 8], F32))
        thrP = E(nc.sbuf_tensor("thrP", [128, 6], F32))
        xres = E(nc.sbuf_tensor("xres", [128, 2048], F32))
        sgt = E(nc.sbuf_tensor("sgt", [128, 2048], BF16))
        w3rt = E(nc.sbuf_tensor("w3rt", [128, 1024], BF16))
        tch = E(nc.sbuf_tensor("tch", [128, 2816], BF16))

        PS = [E(nc.psum_tensor(f"PS{i}", [128, 1024], F32)) for i in range(4)]

        names = (["TE", "VE", "AE", "PE",
                  "XL0", "XL1", "XS0", "XS1", "XF", "CSB", "CSA",
                  "KH0", "KH1", "VH0", "VH1", "KST0", "KST1", "VST0", "VST1",
                  "X2S", "XN2", "CH", "CH1", "USE0", "USE1", "W3R0", "W3R1", "OUT",
                  "WK0", "WK1", "WV0", "WV1", "WQP", "WO0", "WO1",
                  "GL0", "GL1", "QL0", "QL1", "QS0", "QS1"]
                 + [f"XR{i}" for i in range(8)])
        SM = {nm: Sem(E(nc.semaphore(nm))) for nm in names}
        TE, VE, AE, PE = SM["TE"], SM["VE"], SM["AE"], SM["PE"]
        XL = [SM["XL0"], SM["XL1"]]
        XS = [SM["XS0"], SM["XS1"]]
        XF = SM["XF"]
        CSB, CSA = SM["CSB"], SM["CSA"]
        KH = [SM["KH0"], SM["KH1"]]
        VH = [SM["VH0"], SM["VH1"]]
        KST = [SM["KST0"], SM["KST1"]]
        VST = [SM["VST0"], SM["VST1"]]
        X2S, XN2 = SM["X2S"], SM["XN2"]
        CHs = [SM["CH"], SM["CH1"]]
        USE = [SM["USE0"], SM["USE1"]]
        W3R = [SM["W3R0"], SM["W3R1"]]
        OUT = SM["OUT"]
        WKs = [SM["WK0"], SM["WK1"]]
        WVs = [SM["WV0"], SM["WV1"]]
        WQP = SM["WQP"]
        WOs = [SM["WO0"], SM["WO1"]]
        GL = [SM["GL0"], SM["GL1"]]
        QL = [SM["QL0"], SM["QL1"]]
        QS = [SM["QS0"], SM["QS1"]]
        XR = [SM[f"XR{i}"] for i in range(8)]

        block = E(nc.Block())

        b = Build()
        b.auto = {"vector": VE, "scalar": AE, "gpsimd": PE}
        R1f = R1[:].bitcast(F32)
        R2f = R2[:].bitcast(F32)
        R3f = R3[:].bitcast(F32)
        R4f = R4[:].bitcast(F32)

        def ps(i, j):
            return PS[i][:, j * 512:(j + 1) * 512]

        bg = []
        bg_i = [0]

        def pump(n):
            while n > 0 and bg_i[0] < len(bg):
                bg[bg_i[0]]()
                bg_i[0] += 1
                n -= 1

        def pump_all():
            pump(len(bg) - bg_i[0])

        ntile_w = [44, 44, 64]
        width_w = [2048, 2048, 1408]
        WMAT = [W1, W2, W3]
        WQN = [w1qn, w2qn, w3qn]

        def wsrc(mi, g):
            if mi < 2:
                return WMAT[mi][g * 128:(g + 1) * 128, :]
            rt, cc = divmod(g, 4)
            return W3[rt * 128:(rt + 1) * 128, cc * 1408:(cc + 1) * 1408]

        def wqdst(mi, g):
            if mi < 2:
                return WQN[mi][g * 128:(g + 1) * 128, :]
            rt, cc = divmod(g, 4)
            return w3qn[rt * 128:(rt + 1) * 128, cc * 1408:(cc + 1) * 1408]

        gq = {
            "gred": {}, "pool_red_done": [0, 0, 0],
            "thr_ae": [0, 0, 0],
            "qsub": {}, "qstore": {},
            "qstore_done": [None, None, None],
            "chain_w": [[0, 0, 0, 0], [0, 0, 0, 0]],
        }

        def mk_gamma_job(mi, g):
            def job():
                sl = g % 2
                wd = width_w[mi]
                gi = (0, 44, 88)[mi] + g
                prev = None
                if g >= 2:
                    prev = gq["gred"].get((mi, g - 2))
                elif mi > 0:
                    prev = gq["gred"].get((mi - 1, ntile_w[mi - 1] - 2 + g))
                if prev:
                    b.wait("gpsimd", AE, prev)
                t = b.dma("gpsimd", GL[sl], R4f[:, sl * 2048: sl * 2048 + wd],
                          wsrc(mi, g))
                b.wait("scalar", GL[sl], t)
                gq["gred"][(mi, g)] = b.op(
                    "scalar",
                    lambda e, _s=sl, _w=wd, _c=gi: e.activation(
                        R4[:, 8192 + (_s % 2) * 4096: 8192 + (_s % 2) * 4096 + _w],
                        R4f[:, _s * 2048: _s * 2048 + _w], AF.Abs,
                        accum_out=gpart[:, _c:_c + 1]),
                    sem=AE)
                if g == ntile_w[mi] - 1:
                    gq["pool_red_done"][mi] = AE.n
            return job

        def mk_quant_job(mi, g):
            def job():
                sl = g % 2
                wd = width_w[mi]
                if g >= 2:
                    b.wait("gpsimd", VE, gq["qsub"][(mi, g - 2)])
                    b.wait("gpsimd", QS[sl], gq["qstore"][(mi, g - 2)])
                elif mi > 0:
                    pg = ntile_w[mi - 1] - 2 + g
                    b.wait("gpsimd", VE, gq["qsub"][(mi - 1, pg)])
                    b.wait("gpsimd", QS[sl], gq["qstore"][(mi - 1, pg)])
                else:
                    b.wait("gpsimd", AE, gq["pool_red_done"][2])
                t = b.dma("gpsimd", QL[sl], R4f[:, sl * 2048: sl * 2048 + wd],
                          wsrc(mi, g))
                a_ap = R4[:, 8192 + sl * 4096: 8192 + sl * 4096 + wd]
                b_ap = R4[:, 10240 + sl * 4096: 10240 + sl * 4096 + wd]
                o_ap = R4[:, sl * 4096: sl * 4096 + wd]
                w_ap = R4f[:, sl * 2048: sl * 2048 + wd]
                b.wait("gpsimd", QL[sl], t)
                b.wait("gpsimd", AE, gq["thr_ae"][mi])
                b.op("gpsimd", lambda e, _w=w_ap, _a=a_ap, _mi=mi:
                     e.tensor_single_scalar(
                         _a, _w, thrP[:, 2 * _mi:2 * _mi + 1], ALU.is_gt))
                pb = b.op("gpsimd", lambda e, _w=w_ap, _b=b_ap, _mi=mi:
                          e.tensor_single_scalar(
                              _b, _w, thrP[:, 2 * _mi + 1:2 * _mi + 2],
                              ALU.is_lt), sem=PE)
                b.wait("vector", PE, pb)
                gq["qsub"][(mi, g)] = b.op(
                    "vector", lambda e, _a=a_ap, _b=b_ap, _o=o_ap:
                    e.tensor_sub(_o, _a, _b), sem=VE)
                b.wait("gpsimd", VE, gq["qsub"][(mi, g)])
                gq["qstore"][(mi, g)] = b.dma("gpsimd", QS[sl],
                                              wqdst(mi, g), o_ap)
                if g == ntile_w[mi] - 1:
                    gq["qstore_done"][mi] = (QS[0].n, QS[1].n)
            return job

        def mk_chain_job(w, hh, d):
            def job():
                sl = d % 2
                base = sl * 1408
                wqn_t = (w1qn, w2qn)[w]
                wqt_t = (w1qt, w2qt)[w]
                if hh == 0 and d == 0:
                    q0_, q1_ = gq["qstore_done"][w]
                    b.wait("sync", QS[0], q0_)
                    b.wait("sync", QS[1], q1_)
                b.wait("sync", CHs[sl], CHs[sl].n)
                t = b.dma("sync", CHs[sl], tch[:, base:base + 1408],
                          wqn_t[hh * 1408:(hh + 1) * 1408,
                                d * 128:(d + 1) * 128],
                          transpose=True)
                b.wait("sync", CHs[sl], t)
                b.dma("sync", CHs[sl],
                      wqt_t[d * 128:(d + 1) * 128,
                            hh * 1408:(hh + 1) * 1408],
                      tch[:, base:base + 1408])
                if d == 15:
                    gq["chain_w"][w][hh] = (CHs[0].n, CHs[1].n)
            return job

        for mi in range(3):
            for g in range(ntile_w[mi]):
                bg.append(mk_gamma_job(mi, g))
        for g in range(ntile_w[0]):
            bg.append(mk_quant_job(0, g))
        for hh in range(4):
            for d in range(16):
                bg.append(mk_chain_job(0, hh, d))
        for g in range(ntile_w[1]):
            bg.append(mk_quant_job(1, g))
        for hh in range(4):
            for d in range(16):
                bg.append(mk_chain_job(1, hh, d))
        for g in range(ntile_w[2]):
            bg.append(mk_quant_job(2, g))

        b.op("gpsimd", lambda e: e.memset(onec[:], 1.0))
        b.op("gpsimd", lambda e: e.memset(onecf[:], 1.0))
        b.op("gpsimd", lambda e: e.memset(onerf[:], 1.0))
        b.op("gpsimd", lambda e: e.memset(gpart[:], 0.0), sem=PE)
        for i in range(2):
            b.dma("sync", CSB, R2f[:, i * 2048:(i + 1) * 2048],
                  COSB[i * 128:(i + 1) * 128, :])
        for i in range(2):
            b.dma("sync", CSB, R2f[:, 4096 + i * 2048: 4096 + (i + 1) * 2048],
                  SINB[i * 128:(i + 1) * 128, :])
        csb_total = CSB.n

        xt_rel = [0, 0]
        st_store = [0] * 16
        for i in range(16):
            sl = i % 2
            p = sl * 8
            if i >= 2:
                b.wait("sync", VE, xt_rel[sl])
            xl = b.dma("sync", XL[sl], R3f[:, sl * 2048:(sl + 1) * 2048],
                       X[i * 128:(i + 1) * 128, :])
            xt = R3f[:, sl * 2048:(sl + 1) * 2048]
            stg = R1[:, sl * 2048:(sl + 1) * 2048]
            b.wait("vector", XL[sl], xl)
            if i >= 2:
                b.wait("vector", XS[sl], st_store[i - 2])
            b.op("vector", lambda e, _x=xt, _p=p: e.reduce_sum(
                sml[:, _p:_p + 1], _x, axis=AX.X))
            b.op("vector", lambda e, _x=xt, _p=p, _st=stg:
                 e.scalar_tensor_tensor(_st, _x, 1.0, _x, ALU.mult, ALU.mult,
                                        accum_out=sml[:, _p + 1:_p + 2]))
            b.op("vector", lambda e, _p=p: e.tensor_scalar(
                sml[:, _p + 2:_p + 3], sml[:, _p:_p + 1], 1.0 / D, 0.0,
                ALU.mult, ALU.add))
            b.op("vector", lambda e, _p=p: e.tensor_tensor(
                sml[:, _p + 3:_p + 4], sml[:, _p + 2:_p + 3],
                sml[:, _p + 2:_p + 3], ALU.mult))
            b.op("vector", lambda e, _p=p: e.scalar_tensor_tensor(
                sml[:, _p + 4:_p + 5], sml[:, _p + 1:_p + 2], 1.0 / D,
                sml[:, _p + 3:_p + 4], ALU.mult, ALU.subtract))
            vvar = b.op("vector", lambda e, _p=p: e.tensor_scalar(
                sml[:, _p + 4:_p + 5], sml[:, _p + 4:_p + 5], 1.0, 1e-5,
                ALU.mult, ALU.add), sem=VE)
            b.wait("scalar", VE, vvar)
            b.op("scalar", lambda e, _p=p: e.activation(
                sml[:, _p + 5:_p + 6], sml[:, _p + 4:_p + 5], AF.Ln))
            arstd = b.op("scalar", lambda e, _p=p: e.activation(
                sml[:, _p + 6:_p + 7], sml[:, _p + 5:_p + 6], AF.Exp,
                scale=-0.5), sem=AE)
            b.wait("vector", AE, arstd)
            vapp = b.op("vector", lambda e, _x=xt, _p=p, _st=stg:
                        e.tensor_scalar(_st, _x, sml[:, _p + 2:_p + 3],
                                        sml[:, _p + 6:_p + 7],
                                        ALU.subtract, ALU.mult), sem=VE)
            xt_rel[sl] = vapp
            b.wait("sync", VE, vapp)
            st_store[i] = b.dma("sync", XS[sl],
                                xn1s[i * 128:(i + 1) * 128, :], stg)
            pump(2)
        xs_totals = (XS[0].n, XS[1].n)

        b.wait("sync", XS[0], xs_totals[0])
        b.wait("sync", XS[1], xs_totals[1])
        for d in range(16):
            b.dma("sync", XF, R1[:, d * 2048:(d + 1) * 2048],
                  xn1s[:, d * 128:(d + 1) * 128], transpose=True)
        xf_fm = XF.n

        def xn1fm(d, c0, w):
            return R1[:, d * 2048 + c0: d * 2048 + c0 + w]

        wkp_load = [0, 0]
        kgrp_mm = {}
        krope_ve = {}
        kst_tgt = {}
        b.wait("tensor", XF, xf_fm)
        b.wait("vector", CSB, csb_total)
        for j in range(8):
            jsl = j % 2
            if j == 0:
                b.wait("gpsimd", VE, max(xt_rel))
            if j >= 2:
                b.wait("gpsimd", TE, kgrp_mm[(j - 2, 1)])
            wkp_load[jsl] = b.dma(
                "gpsimd", WKs[jsl],
                R3[:, jsl * 4096: jsl * 4096 + 4096].rearrange(
                    "p (c f) -> p c f", f=256),
                r3(Wk)[:, :, 256 * j:256 * (j + 1)])
            for th in range(2):
                par = (2 * j + th) % 2
                c0ps, c1ps = PS[2 * par], PS[2 * par + 1]
                b.wait("tensor", WKs[jsl], wkp_load[jsl])
                prev = (j - 1, th)
                if prev in krope_ve:
                    b.wait("tensor", VE, krope_ve[prev])
                for kc in range(2):
                    for n in range(2):
                        for dd in range(16):
                            b.op("tensor", lambda e, _kc=kc, _n=n, _d=dd,
                                 _jsl=jsl, _th=th,
                                 _ps=(c0ps, c1ps)[kc]: e.matmul(
                                _ps[:, _n * 512:(_n + 1) * 512],
                                R3[:, _jsl * 4096 + _d * 256 + _kc * 128:
                                   _jsl * 4096 + _d * 256 + _kc * 128 + 128],
                                xn1fm(_d, _th * 1024 + _n * 512, 512),
                                start=(_d == 0), stop=(_d == 15)),
                                sem=(TE if (kc == 1 and n == 1 and dd == 15)
                                     else None))
                kgrp_mm[(j, th)] = TE.n
                ksl = (2 * j + th) % 2
                kst = R3[:, 8192 + ksl * 2048: 8192 + (ksl + 1) * 2048]
                b.wait("vector", TE, kgrp_mm[(j, th)])
                prevk = (j - 1, th)
                if prevk in kst_tgt:
                    b.wait("vector", KST[ksl], kst_tgt[prevk])
                cA = R2f[:, th * 1024: th * 1024 + 1024]
                cB = R2f[:, 2048 + th * 1024: 2048 + th * 1024 + 1024]
                sA = R2f[:, 4096 + th * 1024: 4096 + th * 1024 + 1024]
                sB = R2f[:, 6144 + th * 1024: 6144 + th * 1024 + 1024]
                b.op("vector", lambda e, _c=c0ps, _cA=cA: e.tensor_tensor(
                    tmp[:, 0:1024], _c[:, 0:1024], _cA, ALU.mult))
                b.op("vector", lambda e, _c=c1ps, _sA=sA: e.tensor_tensor(
                    tmp[:, 1024:2048], _c[:, 0:1024], _sA, ALU.mult))
                b.op("vector", lambda e, _k=kst: e.tensor_sub(
                    _k[:, 0:1024], tmp[:, 0:1024], tmp[:, 1024:2048]))
                b.op("vector", lambda e, _c=c1ps, _cB=cB: e.tensor_tensor(
                    tmp[:, 0:1024], _c[:, 0:1024], _cB, ALU.mult))
                b.op("vector", lambda e, _c=c0ps, _sB=sB: e.tensor_tensor(
                    tmp[:, 1024:2048], _c[:, 0:1024], _sB, ALU.mult))
                krope_ve[(j, th)] = b.op("vector", lambda e, _k=kst:
                                         e.tensor_add(_k[:, 1024:2048],
                                                      tmp[:, 0:1024],
                                                      tmp[:, 1024:2048]),
                                         sem=VE)
                b.wait("sync", VE, krope_ve[(j, th)])
                t = 0
                for kc in range(2):
                    t = b.dma("sync", KST[ksl],
                              Ks[(2 * j + kc) * 128:(2 * j + kc + 1) * 128,
                                 th * 1024:(th + 1) * 1024],
                              kst[:, kc * 1024:(kc + 1) * 1024])
                kst_tgt[(j, th)] = t
                pump(4)
        kst_totals = (KST[0].n, KST[1].n)
        ve_kropes = VE.n

        def emit_thr(mi):
            b.wait("tensor", AE, gq["pool_red_done"][mi])
            if mi >= 1:
                b.wait("tensor", AE, gq["thr_ae"][mi - 1])
            base = (0, 44, 88)[mi]
            wdt = (44, 44, 64)[mi]
            tcr = b.op("tensor", lambda e, _b=base, _w=wdt: e.matmul(
                PS[3][0:1, 512:512 + _w], onecf[:, 0:1],
                gpart[:, _b:_b + _w], start=True, stop=True), sem=TE)
            b.wait("vector", TE, tcr)
            b.op("vector", lambda e, _mi=mi, _w=wdt: e.reduce_sum(
                gsml[0:1, _mi:_mi + 1], PS[3][0:1, 512:512 + _w],
                axis=AX.X))
            b.op("vector", lambda e, _mi=mi: e.tensor_scalar(
                gsml[0:1, 3 + _mi:4 + _mi], gsml[0:1, _mi:_mi + 1],
                0.5 / NW, 0.5e-8, ALU.mult, ALU.add))
            vthr = b.op("vector", lambda e, _mi=mi: e.tensor_scalar(
                gsml[0:1, 7:8], gsml[0:1, 3 + _mi:4 + _mi], -1.0, 0.0,
                ALU.mult, ALU.add), sem=VE)
            b.wait("tensor", VE, vthr)
            b.op("tensor", lambda e, _mi=mi: e.matmul(
                PS[3][:, 512:513], onerf[0:1, :], gsml[0:1, 3 + _mi:4 + _mi],
                start=True, stop=True))
            tbc = b.op("tensor", lambda e: e.matmul(
                PS[3][:, 513:514], onerf[0:1, :], gsml[0:1, 7:8],
                start=True, stop=True), sem=TE)
            b.wait("scalar", TE, tbc)
            gq["thr_ae"][mi] = b.op("scalar", lambda e, _mi=mi: e.activation(
                thrP[:, 2 * _mi:2 * _mi + 2], PS[3][:, 512:514], AF.Copy),
                sem=AE)

        b.wait("tensor", VE, ve_kropes)
        pump(max(0, 88 - bg_i[0]))
        emit_thr(0)
        emit_thr(1)

        wvq_load = [0, 0]
        v_mm = {}
        v_evac = {}
        vst_tgt = {}
        for fq in range(4):
            fsl = fq % 2
            if fq == 0:
                b.wait("gpsimd", TE, kgrp_mm[(7, 1)])
            if fq == 1:
                b.wait("gpsimd", KST[0], kst_totals[0])
                b.wait("gpsimd", KST[1], kst_totals[1])
            if fq >= 2:
                b.wait("gpsimd", TE, v_mm[(fq - 2, 15)])
            wvq_load[fsl] = b.dma(
                "gpsimd", WVs[fsl],
                R3[:, fsl * 8192: fsl * 8192 + 8192].rearrange(
                    "p (c f) -> p c f", f=512),
                r3(Wv)[:, :, 512 * fq:512 * (fq + 1)])
            b.wait("tensor", WVs[fsl], wvq_load[fsl])
            for tt in range(16):
                vpar = tt % 2
                prev = (fq, tt - 2) if tt >= 2 else \
                    ((fq - 1, 14 + tt) if fq else None)
                if prev in v_evac:
                    b.wait("tensor", AE, v_evac[prev])
                for dd in range(16):
                    b.op("tensor", lambda e, _d=dd, _tt=tt, _fsl=fsl,
                         _vp=vpar: e.matmul(
                        ps(0, _vp),
                        xn1fm(_d, _tt * 128, 128),
                        R3[:, _fsl * 8192 + _d * 512:
                           _fsl * 8192 + (_d + 1) * 512],
                        start=(_d == 0), stop=(_d == 15)),
                        sem=(TE if dd == 15 else None))
                v_mm[(fq, tt)] = TE.n
                b.wait("scalar", TE, v_mm[(fq, tt)])
                gi = fq * 4 + (tt // 4)
                if tt % 4 == 0:
                    pv = vst_tgt.get(gi - 2)
                    if pv:
                        b.wait("scalar", VST[gi % 2], pv)
                gh = (gi % 2) * 2048
                v_evac[(fq, tt)] = b.op(
                    "scalar", lambda e, _tt=tt, _vp=vpar, _gh=gh:
                    e.activation(
                        qp[:, _gh + (_tt % 4) * 512:
                           _gh + ((_tt % 4) + 1) * 512],
                        ps(0, _vp), AF.Copy), sem=AE)
                if tt % 4 == 3:
                    grp = tt // 4
                    b.wait("sync", AE, v_evac[(fq, tt)])
                    vst_tgt[gi] = b.dma(
                        "sync", VST[gi % 2],
                        Vs2[fq * 2048 + grp * 512:
                            fq * 2048 + (grp + 1) * 512,
                            :].rearrange("(t p) f -> p t f", p=128),
                        qp[:, gh:gh + 2048].rearrange(
                            "p (t f) -> p t f", f=512))
                pump(2)
        vst_totals = (VST[0].n, VST[1].n)
        te_vdone = TE.n
        pump(max(0, 152 - bg_i[0]))
        b.wait("tensor", AE, gq["thr_ae"][1])
        emit_thr(2)

        b.wait("sync", VE, ve_kropes)
        b.dma("sync", CSA, R2[:, 0:8192].rearrange("p (m f) -> p m f", f=512),
              MSK.rearrange("(m p) f -> p m f", p=128))
        for i, (srct, off) in enumerate([(COSB, 4096), (COSB, 5120),
                                         (SINB, 6144), (SINB, 7168)]):
            half = i % 2
            b.dma("sync", CSA, R2f[:, off:off + 1024],
                  srct[half * 128:(half + 1) * 128, 0:1024])
        csa_total = CSA.n

        def y_blk(h, qhl, vc):
            blk = 4 * h + 2 * qhl + vc
            return R1[:, blk * 512:(blk + 1) * 512]

        kh_load = {}
        vh_load = {}
        wqp_load = {}
        qbuild_te = {}
        qrope_ve = {}
        ev_pv = {}
        ev_tanh = {}
        ev_exp = {}
        ev_mask = {}
        ynorm_ve = {}
        wo_stop = {}
        wo_evac = {}
        x2store = {}
        xn2store = {}
        first_vec_att = [True]
        last_ynorm = [0]
        ve_after_ln2 = [0, 0]
        bcast_ae = {}

        for s in range(2):
            EV = EA if s == 0 else EB
            nev = len(EV)
            if s == 0:
                b.wait("sync", TE, te_vdone)
            if s == 1:
                b.wait("sync", TE, qbuild_te[(0, 7)])
                b.wait("sync", X2S, x2store[0])
                b.wait("sync", VE, ve_after_ln2[0])
            for d in range(16):
                b.dma("sync", XF,
                      R1[:, 16384 + d * 512: 16384 + (d + 1) * 512],
                      xn1s[s * 512:(s + 1) * 512, d * 128:(d + 1) * 128],
                      transpose=True)
            xf_sl = XF.n

            for h in range(8):
                hsl = h % 2
                if (s, h) == (0, 0):
                    b.wait("sync", KST[0], kst_totals[0])
                    b.wait("sync", KST[1], kst_totals[1])
                    b.wait("sync", VST[0], vst_totals[0])
                    b.wait("sync", VST[1], vst_totals[1])
                rel = ev_pv.get((s, h - 2, nev - 1)) or \
                    (ev_pv.get((0, 6 + h, 7)) if s == 1 else None)
                if rel:
                    b.wait("sync", TE, rel)
                if (s, h) in [(0, 0), (0, 1)]:
                    b.wait("sync", TE, te_vdone)
                if (s, h) in [(1, 0), (1, 1)]:
                    b.wait("sync", TE, wo_stop[(0, 3)])
                kh_load[(s, h)] = b.dma(
                    "sync", KH[hsl],
                    R3[:, 8192 + hsl * 4096:
                       8192 + (hsl + 1) * 4096].rearrange(
                        "p (c f) -> p c f", f=2048),
                    r3(Ks)[:, 2 * h:2 * h + 2, :])
                vh_load[(s, h)] = b.dma(
                    "sync", VH[hsl],
                    R1[:, 24576 + hsl * 4096:
                       24576 + (hsl + 1) * 4096].rearrange(
                        "p (c f) -> p c f", f=256),
                    r3(Vs2[(h // 2) * 2048:(h // 2 + 1) * 2048,
                           (h % 2) * 256:(h % 2) * 256 + 256]))
                relq = qbuild_te.get((s, h - 1)) or \
                    (wo_stop.get((0, 3)) if s == 1 and h == 0 else None)
                if relq:
                    b.wait("gpsimd", TE, relq)
                elif (s, h) == (0, 0):
                    b.wait("gpsimd", TE, te_vdone)
                wqp_load[(s, h)] = b.dma(
                    "gpsimd", WQP,
                    R3[:, 0:8192].rearrange("p (c f) -> p c f", f=512),
                    r3(Wq)[:, :, 512 * h:512 * (h + 1)])

                b.wait("tensor", WQP, wqp_load[(s, h)])
                b.wait("tensor", XF, xf_sl)
                if (s, h) == (0, 0):
                    b.wait("tensor", CSA, csa_total)
                relb = ynorm_ve.get((s, h - 1)) or \
                    (ynorm_ve.get((0, 7)) if s == 1 and h == 0 else None)
                if relb:
                    b.wait("tensor", VE, relb)
                for pr in range(2):
                    for qc in range(2):
                        for dd in range(16):
                            b.op("tensor", lambda e, _pr=pr, _qc=qc,
                                 _d=dd: e.matmul(
                                ps(1 + _pr, _qc),
                                R3[:, _d * 512 + (_pr * 2 + _qc) * 128:
                                   _d * 512 + (_pr * 2 + _qc) * 128 + 128],
                                R1[:, 16384 + _d * 512:
                                   16384 + (_d + 1) * 512],
                                start=(_d == 0), stop=(_d == 15)),
                                sem=(TE if (qc == 1 and dd == 15) else None))
                    qb = TE.n
                    if pr == 1:
                        qbuild_te[(s, h)] = qb
                    if first_vec_att[0]:
                        b.wait("vector", CSA, csa_total)
                        b.wait("vector", VST[0], vst_totals[0])
                        b.wait("vector", VST[1], vst_totals[1])
                        first_vec_att[0] = False
                    if h == 0 and pr == 0:
                        for _xi in range(4):
                            b.wait("vector", XR[_xi], XR[_xi].n)
                    b.wait("vector", TE, qb)
                    qo = hsl * 2048 + pr * 1024
                    sfx = s * 512
                    cA = R2f[:, 4096 + sfx: 4096 + sfx + 512]
                    cB = R2f[:, 5120 + sfx: 5120 + sfx + 512]
                    sA = R2f[:, 6144 + sfx: 6144 + sfx + 512]
                    sB = R2f[:, 7168 + sfx: 7168 + sfx + 512]
                    b.op("vector", lambda e, _pr=pr, _cA=cA: e.tensor_tensor(
                        xres[:, 0:512], ps(1 + _pr, 0), _cA, ALU.mult))
                    b.op("vector", lambda e, _pr=pr, _sA=sA: e.tensor_tensor(
                        xres[:, 512:1024], ps(1 + _pr, 1), _sA, ALU.mult))
                    b.op("vector", lambda e, _qo=qo: e.tensor_sub(
                        qp[:, _qo:_qo + 512], xres[:, 0:512],
                        xres[:, 512:1024]))
                    b.op("vector", lambda e, _pr=pr, _cB=cB: e.tensor_tensor(
                        xres[:, 0:512], ps(1 + _pr, 1), _cB, ALU.mult))
                    b.op("vector", lambda e, _pr=pr, _sB=sB: e.tensor_tensor(
                        xres[:, 512:1024], ps(1 + _pr, 0), _sB, ALU.mult))
                    qrope_ve[(s, h, pr)] = b.op(
                        "vector", lambda e, _qo=qo: e.tensor_add(
                            qp[:, _qo + 512:_qo + 1024],
                            xres[:, 0:512], xres[:, 512:1024]), sem=VE)
                pump(2)

                b.wait("tensor", VE, qrope_ve[(s, h, 1)])
                b.wait("tensor", KH[hsl], kh_load[(s, h)])
                b.wait("tensor", VH[hsl], vh_load[(s, h)])
                qk_te = {}

                def emit_qk(ei, s=s, h=h, hsl=hsl, EV=EV):
                    stile = EV[ei]
                    if ei >= 1:
                        b.wait("tensor", AE, ev_tanh[(s, h, ei - 1)])
                    elif (s, h) != (0, 0):
                        ph = (s, h - 1) if h else (0, 7)
                        b.wait("tensor", AE, bcast_ae[ph])
                    for kc in range(2):
                        for qhl in range(2):
                            b.op("tensor", lambda e, _kc=kc, _q=qhl,
                                 _st=stile, _hsl=hsl: e.matmul(
                                ps(0, _q),
                                R3[:, 8192 + _hsl * 4096 + _kc * 2048 +
                                   _st * 128: 8192 + _hsl * 4096 +
                                   _kc * 2048 + _st * 128 + 128],
                                qp[:, _hsl * 2048 + _q * 1024 + _kc * 512:
                                   _hsl * 2048 + _q * 1024 +
                                   (_kc + 1) * 512],
                                start=(_kc == 0), stop=(_kc == 1)),
                                sem=(TE if (kc == 1 and qhl == 1) else None))
                    qk_te[ei] = TE.n

                def emit_softcap(ei, s=s, h=h, nev=nev):
                    par = ei % 2
                    b.wait("scalar", TE, qk_te[ei])
                    b.op("scalar", lambda e, _par=par: e.activation(
                        tmp[:, _par * 1024: _par * 1024 + 512], ps(0, 0),
                        AF.Tanh, scale=0.00125))
                    ev_tanh[(s, h, ei)] = b.op(
                        "scalar", lambda e, _par=par: e.activation(
                            tmp[:, _par * 1024 + 512: _par * 1024 + 1024],
                            ps(0, 1), AF.Tanh, scale=0.00125), sem=AE)
                    if ei >= 2:
                        b.wait("scalar", TE, ev_pv[(s, h, ei - 2)])
                    elif (s, h) != (0, 0):
                        ph = (s, h - 1) if h else (0, 7)
                        pe_ = (7 if ph[0] == 0 else 15) - (1 - ei)
                        b.wait("scalar", TE, ev_pv[(ph[0], ph[1], pe_)])
                    if (s, h, ei) == (1, 0, 0):
                        b.wait("scalar", XN2, xn2store[(0, 3)])
                    b.op("scalar", lambda e, _par=par: e.activation(
                        pbuf[:, _par * 1024: _par * 1024 + 512],
                        tmp[:, _par * 1024: _par * 1024 + 512],
                        AF.Exp, scale=50.0))
                    ev_exp[(s, h, ei)] = b.op(
                        "scalar", lambda e, _par=par: e.activation(
                            pbuf[:, _par * 1024 + 512: _par * 1024 + 1024],
                            tmp[:, _par * 1024 + 512: _par * 1024 + 1024],
                            AF.Exp, scale=50.0), sem=AE)

                def emit_mask(ei, s=s, h=h, EV=EV):
                    stile = EV[ei]
                    par = ei % 2
                    masked = (s == 0) or (stile in MASKED_B)
                    if not masked:
                        return
                    midx = ei if s == 0 else \
                        (8 + [4, 5, 6, 7, 12, 14, 15].index(stile)
                         if False else
                         8 + [4, 5, 6, 7, 12, 13, 14, 15].index(stile))
                    b.wait("vector", AE, ev_exp[(s, h, ei)])
                    b.op("vector", lambda e, _par=par, _m=midx:
                         e.tensor_mul(
                             pbuf[:, _par * 1024: _par * 1024 + 512],
                             pbuf[:, _par * 1024: _par * 1024 + 512],
                             R2[:, _m * 512:(_m + 1) * 512]))
                    ev_mask[(s, h, ei)] = b.op(
                        "vector", lambda e, _par=par, _m=midx:
                        e.tensor_mul(
                            pbuf[:, _par * 1024 + 512: _par * 1024 + 1024],
                            pbuf[:, _par * 1024 + 512: _par * 1024 + 1024],
                            R2[:, _m * 512:(_m + 1) * 512]), sem=VE)

                def emit_pv(ei, s=s, h=h, hsl=hsl, EV=EV, nev=nev):
                    stile = EV[ei]
                    par = ei % 2
                    masked = (s == 0) or (stile in MASKED_B)
                    if masked:
                        b.wait("tensor", VE, ev_mask[(s, h, ei)])
                    else:
                        b.wait("tensor", AE, ev_exp[(s, h, ei)])
                    first = (ei == 0)
                    lastE = (ei == nev - 1)
                    for vc in range(2):
                        for qhl in range(2):
                            b.op("tensor", lambda e, _vc=vc, _q=qhl,
                                 _st=stile, _hsl=hsl, _par=par, _f=first,
                                 _l=lastE: e.matmul(
                                ps(1 + _q, _vc),
                                R1[:, 24576 + _hsl * 4096 + _st * 256 +
                                   _vc * 128: 24576 + _hsl * 4096 +
                                   _st * 256 + _vc * 128 + 128],
                                pbuf[:, _par * 1024 + _q * 512:
                                     _par * 1024 + (_q + 1) * 512],
                                start=_f, stop=_l))
                    b.op("tensor", lambda e, _par=par, _f=first, _l=lastE:
                         e.matmul(PS[3][0:1, 0:512], onec[:, 0:1],
                                  pbuf[:, _par * 1024: _par * 1024 + 512],
                                  start=_f, stop=_l))
                    ev_pv[(s, h, ei)] = b.op(
                        "tensor", lambda e, _par=par, _f=first, _l=lastE:
                        e.matmul(PS[3][0:1, 512:1024], onec[:, 0:1],
                                 pbuf[:, _par * 1024 + 512:
                                      _par * 1024 + 1024],
                                 start=_f, stop=_l), sem=TE)

                emit_qk(0)
                emit_softcap(0)
                emit_mask(0)
                for ei in range(nev):
                    if ei + 1 < nev:
                        emit_qk(ei + 1)
                        emit_softcap(ei + 1)
                        emit_mask(ei + 1)
                    emit_pv(ei)
                    if ei % 2 == 1:
                        pump(1)

                b.wait("vector", TE, ev_pv[(s, h, nev - 1)])
                b.op("vector", lambda e: e.reciprocal(
                    tmp[0:1, 1024:1536], PS[3][0:1, 0:512]))
                vr2 = b.op("vector", lambda e: e.reciprocal(
                    tmp[0:1, 1536:2048], PS[3][0:1, 512:1024]), sem=VE)
                b.wait("tensor", VE, vr2)
                b.wait("tensor", AE, ev_tanh[(s, h, nev - 1)])
                b.op("tensor", lambda e: e.matmul(
                    PS[3][:, 512:1024], onerf[0:1, :], tmp[0:1, 1024:1536],
                    start=True, stop=True))
                tb1 = b.op("tensor", lambda e: e.matmul(
                    ps(0, 0), onerf[0:1, :], tmp[0:1, 1536:2048],
                    start=True, stop=True), sem=TE)
                b.wait("scalar", TE, tb1)
                b.wait("scalar", VE, last_ynorm[0])
                b.op("scalar", lambda e: e.activation(
                    bcs[:, 0:512], PS[3][:, 512:1024], AF.Copy))
                ta1 = b.op("scalar", lambda e: e.activation(
                    bcs[:, 512:1024], ps(0, 0), AF.Copy), sem=AE)
                bcast_ae[(s, h)] = ta1
                b.wait("vector", AE, ta1)
                for qhl in range(2):
                    for vc in range(2):
                        b.op("vector", lambda e, _q=qhl, _vc=vc, _h=h:
                             e.tensor_mul(y_blk(_h, _q, _vc),
                                          ps(1 + _q, _vc),
                                          bcs[:, _q * 512:(_q + 1) * 512]),
                             sem=VE)
                ynorm_ve[(s, h)] = VE.n
                last_ynorm[0] = VE.n
                pump(4)

            wo_load = {}
            for nq in range(4):
                for hf in range(2):
                    if nq == 0 and hf == 0:
                        b.wait("gpsimd", TE, ev_pv[(s, 7, nev - 1)])
                    elif nq and hf == 0:
                        b.wait("gpsimd", TE, wo_stop[(s, nq - 1)])
                    wo_load[(nq, hf)] = b.dma(
                        "gpsimd", WOs[hf],
                        R3[:, hf * 8192: hf * 8192 + 8192].rearrange(
                            "p (c f) -> p c f", f=512),
                        r3(Wo[hf * 2048:(hf + 1) * 2048,
                              nq * 512:(nq + 1) * 512]))
                wpar = nq % 2
                b.wait("tensor", VE, ynorm_ve[(s, 7)])
                if nq >= 2:
                    for tt in range(4):
                        b.wait("tensor", VE, wo_evac[(s, nq - 2, tt)])
                for kc in range(32):
                    hf = kc // 16
                    if kc % 16 == 0:
                        b.wait("tensor", WOs[hf], wo_load[(nq, hf)])
                    for tt in range(4):
                        b.op("tensor", lambda e, _kc=kc, _tt=tt, _hf=hf,
                             _wp=wpar: e.matmul(
                            ps(2 * _wp + _tt // 2, _tt % 2),
                            R1[:, _kc * 512 + _tt * 128:
                               _kc * 512 + _tt * 128 + 128],
                            R3[:, _hf * 8192 + (_kc % 16) * 512:
                               _hf * 8192 + (_kc % 16) * 512 + 512],
                            start=(_kc == 0), stop=(_kc == 31)),
                            sem=(TE if (kc == 31 and tt == 3) else None))
                wo_stop[(s, nq)] = TE.n
                for tt in range(4):
                    if nq == 0 and tt == 0:
                        b.wait("sync", VE, qrope_ve[(s, 7, 1)])
                    pv = wo_evac.get((s, nq - 1, tt)) if nq else None
                    if pv:
                        b.wait("sync", VE, pv)
                    xr = b.dma("sync", XR[tt],
                               xres[:, tt * 512:(tt + 1) * 512],
                               X[s * 512 + tt * 128:
                                 s * 512 + (tt + 1) * 128,
                                 nq * 512:(nq + 1) * 512])
                    b.wait("vector", XR[tt], xr)
                    if s == 1 and nq == 0 and tt == 0:
                        b.wait("vector", X2S, x2store[0])
                    b.wait("vector", TE, wo_stop[(s, nq)])
                    wo_evac[(s, nq, tt)] = b.op(
                        "vector", lambda e, _tt=tt, _wp=wpar, _nq=nq:
                        e.tensor_add(
                            R1f[:, 8192 + _tt * 2048 + _nq * 512:
                                8192 + _tt * 2048 + (_nq + 1) * 512],
                            ps(2 * _wp + _tt // 2, _tt % 2),
                            xres[:, _tt * 512:(_tt + 1) * 512]),
                        sem=VE)
                pump(2)

            for tt in range(4):
                for nq in range(4):
                    b.wait("vector", VE, wo_evac[(s, nq, tt)])
                xv = R1f[:, 8192 + tt * 2048: 8192 + (tt + 1) * 2048]
                b.op("vector", lambda e, _x=xv, _tt=tt: e.reduce_sum(
                    ln2m[:, _tt:_tt + 1], _x, axis=AX.X))
                b.op("vector", lambda e, _x=xv, _tt=tt:
                     e.scalar_tensor_tensor(
                         tmp[:, 0:2048], _x, 1.0, _x, ALU.mult, ALU.mult,
                         accum_out=ln2m[:, 4 + _tt:5 + _tt]))
                b.op("vector", lambda e, _tt=tt: e.tensor_scalar(
                    ln2m[:, _tt:_tt + 1], ln2m[:, _tt:_tt + 1], 1.0 / D,
                    0.0, ALU.mult, ALU.add))
                b.op("vector", lambda e, _tt=tt: e.tensor_tensor(
                    ln2m[:, 8 + _tt:9 + _tt], ln2m[:, _tt:_tt + 1],
                    ln2m[:, _tt:_tt + 1], ALU.mult))
                b.op("vector", lambda e, _tt=tt: e.scalar_tensor_tensor(
                    ln2m[:, 4 + _tt:5 + _tt], ln2m[:, 4 + _tt:5 + _tt],
                    1.0 / D, ln2m[:, 8 + _tt:9 + _tt], ALU.mult,
                    ALU.subtract))
                v2 = b.op("vector", lambda e, _tt=tt: e.tensor_scalar(
                    ln2m[:, 4 + _tt:5 + _tt], ln2m[:, 4 + _tt:5 + _tt],
                    1.0, 1e-5, ALU.mult, ALU.add), sem=VE)
                b.wait("scalar", VE, v2)
                b.op("scalar", lambda e, _tt=tt: e.activation(
                    ln2m[:, 4 + _tt:5 + _tt], ln2m[:, 4 + _tt:5 + _tt],
                    AF.Ln))
                a2 = b.op("scalar", lambda e, _tt=tt: e.activation(
                    ln2m[:, 8 + _tt:9 + _tt], ln2m[:, 4 + _tt:5 + _tt],
                    AF.Exp, scale=-0.5), sem=AE)
                b.wait("vector", AE, a2)
                if (s, tt) != (0, 0):
                    prevst = xn2store.get((s, tt - 1)) or xn2store.get((0, 3))
                    if prevst:
                        b.wait("vector", XN2, prevst)
                ap2 = b.op("vector", lambda e, _x=xv, _tt=tt:
                           e.tensor_scalar(
                               pbuf[:, 0:2048], _x, ln2m[:, _tt:_tt + 1],
                               ln2m[:, 8 + _tt:9 + _tt], ALU.subtract,
                               ALU.mult), sem=VE)
                b.wait("sync", VE, ap2)
                xn2store[(s, tt)] = b.dma(
                    "sync", XN2,
                    xn2s[s * 512 + tt * 128: s * 512 + (tt + 1) * 128, :],
                    pbuf[:, 0:2048])
                pump(1)
            ve_after_ln2[s] = VE.n
            b.wait("sync", VE, wo_evac[(s, 3, 3)])
            x2store[s] = b.dma(
                "sync", X2S,
                x2s[s * 512:(s + 1) * 512, :].rearrange(
                    "(t p) f -> p t f", p=128),
                R1f[:, 8192:16384].rearrange("p (t f) -> p t f", f=2048))

        pump_all()
        b.wait("sync", VE, gq["qsub"][(2, ntile_w[2] - 1)])
        q0, q1 = gq["qstore_done"][2]
        b.wait("sync", QS[0], q0)
        b.wait("sync", QS[1], q1)
        b.wait("sync", XN2, xn2store[(1, 3)])
        for d in range(16):
            b.dma("sync", XF, R4[:, d * 1024:(d + 1) * 1024],
                  xn2s[:, d * 128:(d + 1) * 128], transpose=True)
        xf_xn2 = XF.n

        def hid(hc, c0, w):
            if hc < 32:
                return R1[:, hc * 1024 + c0: hc * 1024 + c0 + w]
            return R2[:, (hc - 32) * 1024 + c0: (hc - 32) * 1024 + c0 + w]

        use_tgt = {}
        use_cons = {}
        sig_ae = {}
        hmul_ve = {}
        for hc in range(44):
            par = hc % 2
            hh = min(hc // 11, 3)
            for m in range(2):
                c0_, c1_ = gq["chain_w"][m][hh]
                b.wait("sync", CHs[0], c0_)
                b.wait("sync", CHs[1], c1_)
                prev = use_cons.get(hc - 2)
                if prev:
                    b.wait("sync", TE, prev)
                wqt_t = (w1qt, w2qt)[m]
                use_tgt[(hc, m)] = b.dma(
                    "sync", USE[par],
                    R3[:, m * 4096 + par * 2048:
                       m * 4096 + (par + 1) * 2048].rearrange(
                        "p (c f) -> p c f", f=128),
                    r3(wqt_t)[:, :, hc * 128:(hc + 1) * 128])
            b.wait("tensor", USE[par], use_tgt[(hc, 1)])
            if hc == 0:
                b.wait("tensor", XF, xf_xn2)
            if hc >= 2:
                b.wait("tensor", AE, sig_ae[hc - 2])
                b.wait("tensor", VE, hmul_ve[hc - 2])
            for m in range(2):
                for n in range(2):
                    for dd in range(16):
                        b.op("tensor", lambda e, _m=m, _n=n, _d=dd,
                             _p=par: e.matmul(
                            ps(2 * _p + _m, _n),
                            R3[:, _m * 4096 + _p * 2048 + _d * 128:
                               _m * 4096 + _p * 2048 + (_d + 1) * 128],
                            R4[:, _d * 1024 + _n * 512:
                               _d * 1024 + (_n + 1) * 512],
                            start=(_d == 0), stop=(_d == 15)),
                            sem=(TE if (m == 1 and n == 1 and dd == 15)
                                 else None))
            xte = TE.n
            use_cons[hc] = xte
            b.wait("scalar", TE, xte)
            if hc >= 2:
                b.wait("scalar", VE, hmul_ve[hc - 2])
            for n in range(2):
                sig_ae[hc] = b.op("scalar", lambda e, _n=n, _p=par:
                                  e.activation(
                                      sgt[:, _p * 1024 + _n * 512:
                                          _p * 1024 + (_n + 1) * 512],
                                      ps(2 * _p, _n), AF.Sigmoid), sem=AE)
            b.wait("vector", AE, sig_ae[hc])
            if hc == 0:
                b.wait("vector", X2S, x2store[1])
            for n in range(2):
                hmul_ve[hc] = b.op("vector", lambda e, _n=n, _p=par, _hc=hc:
                                   e.tensor_mul(
                                       hid(_hc, _n * 512, 512),
                                       ps(2 * _p + 1, _n),
                                       sgt[:, _p * 1024 + _n * 512:
                                           _p * 1024 + (_n + 1) * 512]),
                                   sem=VE)
        ve_hidden = VE.n

        w3_tgt = {}
        w3_cons = {}
        w3_evac = {}
        out_store = {}
        for nq in range(4):
            for hc in range(44):
                w3par = hc % 2
                if (nq, hc) == (0, 0):
                    b.wait("sync", QS[0], q0)
                    b.wait("sync", QS[1], q1)
                prev = w3_cons.get((nq, hc - 2)) or \
                    (w3_cons.get((nq - 1, 42 + hc)) if nq else None)
                if prev:
                    b.wait("sync", TE, prev)
                w3_tgt[(nq, hc)] = b.dma(
                    "sync", W3R[w3par],
                    w3rt[:, w3par * 512:(w3par + 1) * 512],
                    w3qn[nq * 512:(nq + 1) * 512, hc * 128:(hc + 1) * 128],
                    transpose=True)
                b.wait("tensor", W3R[w3par], w3_tgt[(nq, hc)])
                if hc == 0:
                    b.wait("tensor", VE, ve_hidden)
                    if nq >= 1:
                        for tt in range(8):
                            b.wait("tensor", VE, w3_evac[(nq - 1, tt)])
                for tt in range(8):
                    b.op("tensor", lambda e, _tt=tt, _hc=hc, _p=w3par:
                         e.matmul(
                             ps(_tt // 2, _tt % 2),
                             hid(_hc, _tt * 128, 128),
                             w3rt[:, _p * 512:(_p + 1) * 512],
                             start=(_hc == 0), stop=(_hc == 43)),
                         sem=(TE if tt == 7 else None))
                w3_cons[(nq, hc)] = TE.n
            w3te = TE.n
            for tt in range(8):
                pv = w3_evac.get((nq - 1, tt)) if nq else None
                if pv:
                    b.wait("sync", VE, pv)
                if nq == 0 and tt == 0:
                    b.wait("sync", X2S, x2store[1])
                    b.wait("sync", VE, wo_evac[(1, 3, 3)])
                    for _xi in range(8):
                        b.wait("sync", XR[_xi], XR[_xi].n)
                if tt >= 4:
                    b.wait("sync", VE, w3_evac[(nq, tt - 4)])
                xr = b.dma("sync", XR[tt],
                           xres[:, (tt % 4) * 512:((tt % 4) + 1) * 512],
                           x2s[tt * 128:(tt + 1) * 128,
                               nq * 512:(nq + 1) * 512])
                b.wait("vector", XR[tt], xr)
                b.wait("vector", TE, w3te)
                if nq and tt == 0:
                    b.wait("vector", OUT, out_store[nq - 1])
                w3_evac[(nq, tt)] = b.op(
                    "vector", lambda e, _tt=tt: e.tensor_add(
                        R3f[:, 4096 + _tt * 512: 4096 + (_tt + 1) * 512],
                        ps(_tt // 2, _tt % 2),
                        xres[:, (_tt % 4) * 512:((_tt % 4) + 1) * 512]),
                    sem=VE)
            b.wait("sync", VE, w3_evac[(nq, 7)])
            out_store[nq] = b.dma(
                "sync", OUT,
                Y.rearrange("(t p) f -> p t f",
                            p=128)[:, :, nq * 512:(nq + 1) * 512],
                R3f[:, 4096:8192].rearrange("p (t f) -> p t f", f=512))
        b.wait("sync", OUT, out_store[3])

        assert bg_i[0] == len(bg), f"unpumped bg jobs: {bg_i[0]}/{len(bg)}"

        @block.tensor
        def _(eng):
            for f in b.q["tensor"]:
                f(eng)

        @block.vector
        def _(eng):
            for f in b.q["vector"]:
                f(eng)

        @block.scalar
        def _(eng):
            for f in b.q["scalar"]:
                f(eng)

        @block.gpsimd
        def _(eng):
            for f in b.q["gpsimd"]:
                f(eng)

        @block.sync
        def _(eng):
            for f in b.q["sync"]:
                f(eng)

    return nc


_NC_CACHE = [None]


def _host_tables():
    inv = 1.0 / (10000.0 ** (np.arange(0, 256, 2, dtype=np.float64) / 256.0))
    return inv


def kernel(X=None, **kw):
    if X is None:
        X = kw.pop("L0_attn_X")
    Wq = kw.pop("L0_attn_Wq")
    Wk = kw.pop("L0_attn_Wk")
    Wv = kw.pop("L0_attn_Wv")
    Wo = kw.pop("L0_attn_Wo")
    W1 = kw.pop("L0_ffn_w1")
    W2 = kw.pop("L0_ffn_w2")
    W3 = kw.pop("L0_ffn_w3")
    X = np.asarray(X, dtype=np.float32)
    B = X.shape[0]

    inv = _host_tables()
    pos_all = np.arange(TB, dtype=np.float64)

    tri = np.zeros((4, 128, 512), dtype=np.float32)
    for j in range(4):
        tri[j] = (np.arange(512)[None, :] >=
                  (np.arange(128)[:, None] + 128 * j))
    ones_t = np.ones((128, 512), np.float32)
    zeros_t = np.zeros((128, 512), np.float32)

    if _NC_CACHE[0] is None:
        _NC_CACHE[0] = build_program()
    nc = _NC_CACHE[0]

    in_maps = []
    metas = []
    for core in range(8):
        bb, sg_ = divmod(core, 2)
        order = [0, 3, 1, 2] if sg_ == 0 else [1, 2, 0, 3]
        rows = np.concatenate([np.arange(c * 512, (c + 1) * 512)
                               for c in order])
        Xp = np.ascontiguousarray(X[bb][rows])
        posg = pos_all[rows]
        ang = inv[:, None] * posg[None, :]
        cosb = np.concatenate([np.cos(ang), np.cos(ang)], 0).astype(np.float32)
        sinb = np.concatenate([np.sin(ang), np.sin(ang)], 0).astype(np.float32)
        mtiles = [tri[j] for j in range(4)]
        mtiles += [zeros_t if sg_ == 0 else ones_t] * 4
        mtiles += [tri[j] for j in range(4)]
        mtiles += [ones_t if sg_ == 0 else zeros_t] * 4
        msk = np.concatenate(mtiles, 0).astype(ml_dtypes.bfloat16)
        in_maps.append({
            "X": Xp, "Wq": np.asarray(Wq, np.float32),
            "Wk": np.asarray(Wk, np.float32),
            "Wv": np.asarray(Wv, np.float32),
            "Wo": np.asarray(Wo, np.float32),
            "W1": np.asarray(W1, np.float32),
            "W2": np.asarray(W2, np.float32),
            "W3": np.asarray(W3, np.float32),
            "COSB": cosb, "SINB": sinb, "MSK": msk,
        })
        metas.append((bb, order))

    res = run_bass_kernel_spmd(nc, in_maps, core_ids=list(range(8)))

    out = np.empty((B, TB, D), dtype=np.float32)
    for core in range(8):
        bb, order = metas[core]
        Yc = np.asarray(res.results[core]["Y"])
        cA, cB = order[0], order[1]
        out[bb, cA * 512:(cA + 1) * 512] = Yc[0:512]
        out[bb, cB * 512:(cB + 1) * 512] = Yc[512:1024]
    return out


# version 03
# speedup vs baseline: 2.3519x; 2.3519x over previous
import contextlib
import numpy as np
import ml_dtypes
import concourse.bass as bass
import concourse.mybir as mybir
from concourse.bass_utils import run_bass_kernel_spmd

F32 = mybir.dt.float32
BF16 = mybir.dt.bfloat16
AF = mybir.ActivationFunctionType
ALU = mybir.AluOpType
AX = mybir.AxisListType

D = 2048
TB = 2048
OWN = 1024
HFF = 5632
NW = float(HFF * D)
EA = [0, 1, 2, 3, 8, 9, 10, 11]
EB = list(range(16))
MASKED_B = {4, 5, 6, 7, 12, 13, 14, 15}


class Sem:
    def __init__(self, h):
        self.h = h
        self.n = 0


class Build:
    ENG = ("tensor", "vector", "scalar", "gpsimd", "sync")

    def __init__(self):
        self.q = {e: [] for e in self.ENG}
        self.auto = {}

    def emit(self, eng, fn):
        self.q[eng].append(fn)

    def wait(self, eng, sem, val):
        if val > 0:
            self.emit(eng, lambda e, _s=sem.h, _v=val: e.wait_ge(_s, _v))

    def dma(self, qeng, sem, dst, src, transpose=False):
        sem.n += 16
        tgt = sem.n

        def f(e, _d=dst, _s=src, _t=transpose, _h=sem.h):
            if _t:
                e.dma_start_transpose(_d, _s).then_inc(_h, 16)
            else:
                e.dma_start(_d, _s).then_inc(_h, 16)

        self.emit(qeng, f)
        return tgt

    def op(self, eng, fn, sem=None, k=1):
        a = self.auto.get(eng)
        if a is not None:
            self.wait(eng, a, a.n)
            if sem is None:
                sem = a
        if sem is None:
            self.emit(eng, lambda e, _f=fn: _f(e))
            return None
        sem.n += k
        tgt = sem.n

        def f(e, _f=fn, _h=sem.h, _k=k):
            _f(e).then_inc(_h, _k)

        self.emit(eng, f)
        return tgt


def build_program():
    nc = bass.Bass("TRN2", target_bir_lowering=False)

    X = nc.dram_tensor("X", [TB, D], F32, kind="ExternalInput").ap()
    Wq = nc.dram_tensor("Wq", [D, 4096], F32, kind="ExternalInput").ap()
    Wk = nc.dram_tensor("Wk", [D, D], F32, kind="ExternalInput").ap()
    Wv = nc.dram_tensor("Wv", [D, D], F32, kind="ExternalInput").ap()
    Wo = nc.dram_tensor("Wo", [4096, D], F32, kind="ExternalInput").ap()
    W1 = nc.dram_tensor("W1", [HFF, D], F32, kind="ExternalInput").ap()
    W2 = nc.dram_tensor("W2", [HFF, D], F32, kind="ExternalInput").ap()
    W3 = nc.dram_tensor("W3", [D, HFF], F32, kind="ExternalInput").ap()
    COSB = nc.dram_tensor("COSB", [256, TB], F32, kind="ExternalInput").ap()
    SINB = nc.dram_tensor("SINB", [256, TB], F32, kind="ExternalInput").ap()
    MSK = nc.dram_tensor("MSK", [16 * 128, 512], BF16, kind="ExternalInput").ap()
    Y = nc.dram_tensor("Y", [OWN, D], F32, kind="ExternalOutput").ap()

    xn1s = nc.dram_tensor("xn1s", [TB, D], BF16, kind="Internal").ap()
    Ks = nc.dram_tensor("Ks", [D, TB], BF16, kind="Internal").ap()
    Vs2 = nc.dram_tensor("Vs2", [4 * TB, 512], BF16, kind="Internal").ap()
    x2s = nc.dram_tensor("x2s", [OWN, D], F32, kind="Internal").ap()
    xn2s = nc.dram_tensor("xn2s", [OWN, D], BF16, kind="Internal").ap()
    w1qn = nc.dram_tensor("w1qn", [HFF, D], BF16, kind="Internal").ap()
    w2qn = nc.dram_tensor("w2qn", [HFF, D], BF16, kind="Internal").ap()
    w3qn = nc.dram_tensor("w3qn", [D, HFF], BF16, kind="Internal").ap()
    w1qt = nc.dram_tensor("w1qt", [D, HFF], BF16, kind="Internal").ap()
    w2qt = nc.dram_tensor("w2qt", [D, HFF], BF16, kind="Internal").ap()

    def r3(t, p=128):
        return t.rearrange("(c p) f -> p c f", p=p)

    with contextlib.ExitStack() as stk:
        E = stk.enter_context
        R1 = E(nc.sbuf_tensor("R1", [128, 32768], BF16))
        R2 = E(nc.sbuf_tensor("R2", [128, 16384], BF16))
        R3 = E(nc.sbuf_tensor("R3", [128, 16384], BF16))
        R4 = E(nc.sbuf_tensor("R4", [128, 16384], BF16))
        qp = E(nc.sbuf_tensor("qp", [128, 4096], BF16))
        pbuf = E(nc.sbuf_tensor("pbuf", [128, 2048], BF16))
        tmp = E(nc.sbuf_tensor("tmp", [128, 2048], F32))
        bcs = E(nc.sbuf_tensor("bcs", [128, 1024], F32))
        onec = E(nc.sbuf_tensor("onec", [128, 1], BF16))
        onecf = E(nc.sbuf_tensor("onecf", [128, 1], F32))
        onerf = E(nc.sbuf_tensor("onerf", [1, 128], F32))
        sml = E(nc.sbuf_tensor("sml", [128, 16], F32))
        ln2m = E(nc.sbuf_tensor("ln2m", [128, 12], F32))
        gpart = E(nc.sbuf_tensor("gpart", [128, 192], F32))
        gsml = E(nc.sbuf_tensor("gsml", [1, 8], F32))
        thrP = E(nc.sbuf_tensor("thrP", [128, 6], F32))
        xres = E(nc.sbuf_tensor("xres", [128, 2048], F32))
        sgt = E(nc.sbuf_tensor("sgt", [128, 2048], BF16))
        w3rt = E(nc.sbuf_tensor("w3rt", [128, 1024], BF16))
        tch = E(nc.sbuf_tensor("tch", [128, 2816], BF16))

        PS = [E(nc.psum_tensor(f"PS{i}", [128, 1024], F32)) for i in range(4)]

        names = (["TE", "VE", "AE", "PE",
                  "XL0", "XL1", "XS0", "XS1", "XF", "CSB", "CSA",
                  "KH0", "KH1", "VH0", "VH1", "KST0", "KST1", "VST0", "VST1",
                  "X2S", "XN2", "CH", "CH1", "USE0", "USE1", "W3R0", "W3R1", "OUT",
                  "WK0", "WK1", "WV0", "WV1", "WQP", "WO0", "WO1",
                  "GL0", "GL1", "QL0", "QL1", "QS0", "QS1"]
                 + [f"XR{i}" for i in range(8)])
        SM = {nm: Sem(E(nc.semaphore(nm))) for nm in names}
        TE, VE, AE, PE = SM["TE"], SM["VE"], SM["AE"], SM["PE"]
        XL = [SM["XL0"], SM["XL1"]]
        XS = [SM["XS0"], SM["XS1"]]
        XF = SM["XF"]
        CSB, CSA = SM["CSB"], SM["CSA"]
        KH = [SM["KH0"], SM["KH1"]]
        VH = [SM["VH0"], SM["VH1"]]
        KST = [SM["KST0"], SM["KST1"]]
        VST = [SM["VST0"], SM["VST1"]]
        X2S, XN2 = SM["X2S"], SM["XN2"]
        CHs = [SM["CH"], SM["CH1"]]
        USE = [SM["USE0"], SM["USE1"]]
        W3R = [SM["W3R0"], SM["W3R1"]]
        OUT = SM["OUT"]
        WKs = [SM["WK0"], SM["WK1"]]
        WVs = [SM["WV0"], SM["WV1"]]
        WQP = SM["WQP"]
        WOs = [SM["WO0"], SM["WO1"]]
        GL = [SM["GL0"], SM["GL1"]]
        QL = [SM["QL0"], SM["QL1"]]
        QS = [SM["QS0"], SM["QS1"]]
        XR = [SM[f"XR{i}"] for i in range(8)]

        block = E(nc.Block())

        b = Build()
        b.auto = {"vector": VE, "scalar": AE, "gpsimd": PE}
        R1f = R1[:].bitcast(F32)
        R2f = R2[:].bitcast(F32)
        R3f = R3[:].bitcast(F32)
        R4f = R4[:].bitcast(F32)

        def ps(i, j):
            return PS[i][:, j * 512:(j + 1) * 512]

        bg = []
        bg_i = [0]

        def pump(n):
            while n > 0 and bg_i[0] < len(bg):
                bg[bg_i[0]]()
                bg_i[0] += 1
                n -= 1

        def pump_all():
            pump(len(bg) - bg_i[0])

        ntile_w = [44, 44, 64]
        width_w = [2048, 2048, 1408]
        WMAT = [W1, W2, W3]
        WQN = [w1qn, w2qn, w3qn]

        def wsrc(mi, g):
            if mi < 2:
                return WMAT[mi][g * 128:(g + 1) * 128, :]
            rt, cc = divmod(g, 4)
            return W3[rt * 128:(rt + 1) * 128, cc * 1408:(cc + 1) * 1408]

        def wqdst(mi, g):
            if mi < 2:
                return WQN[mi][g * 128:(g + 1) * 128, :]
            rt, cc = divmod(g, 4)
            return w3qn[rt * 128:(rt + 1) * 128, cc * 1408:(cc + 1) * 1408]

        gq = {
            "gred": {}, "pool_red_done": [0, 0, 0],
            "thr_ae": [0, 0, 0],
            "qsub": {}, "qstore": {},
            "qstore_done": [None, None, None],
            "chain_w": [[0, 0, 0, 0], [0, 0, 0, 0]],
        }

        def mk_gamma_job(mi, g):
            def job():
                sl = g % 2
                wd = width_w[mi]
                gi = (0, 44, 88)[mi] + g
                prev = None
                if g >= 2:
                    prev = gq["gred"].get((mi, g - 2))
                elif mi > 0:
                    prev = gq["gred"].get((mi - 1, ntile_w[mi - 1] - 2 + g))
                if prev:
                    b.wait("gpsimd", AE, prev)
                t = b.dma("gpsimd", GL[sl], R4f[:, sl * 2048: sl * 2048 + wd],
                          wsrc(mi, g))
                b.wait("scalar", GL[sl], t)
                gq["gred"][(mi, g)] = b.op(
                    "scalar",
                    lambda e, _s=sl, _w=wd, _c=gi: e.activation(
                        R4[:, 8192 + (_s % 2) * 4096: 8192 + (_s % 2) * 4096 + _w],
                        R4f[:, _s * 2048: _s * 2048 + _w], AF.Abs,
                        accum_out=gpart[:, _c:_c + 1]),
                    sem=AE)
                if g == ntile_w[mi] - 1:
                    gq["pool_red_done"][mi] = AE.n
            return job

        def mk_quant_job(mi, g):
            def job():
                sl = g % 2
                wd = width_w[mi]
                if g >= 2:
                    b.wait("gpsimd", VE, gq["qsub"][(mi, g - 2)])
                    b.wait("gpsimd", QS[sl], gq["qstore"][(mi, g - 2)])
                elif mi > 0:
                    pg = ntile_w[mi - 1] - 2 + g
                    b.wait("gpsimd", VE, gq["qsub"][(mi - 1, pg)])
                    b.wait("gpsimd", QS[sl], gq["qstore"][(mi - 1, pg)])
                else:
                    b.wait("gpsimd", AE, gq["pool_red_done"][2])
                t = b.dma("gpsimd", QL[sl], R4f[:, sl * 2048: sl * 2048 + wd],
                          wsrc(mi, g))
                a_ap = R4[:, 8192 + sl * 4096: 8192 + sl * 4096 + wd]
                b_ap = R4[:, 10240 + sl * 4096: 10240 + sl * 4096 + wd]
                o_ap = R4[:, sl * 4096: sl * 4096 + wd]
                w_ap = R4f[:, sl * 2048: sl * 2048 + wd]
                b.wait("vector", QL[sl], t)
                b.wait("vector", AE, gq["thr_ae"][mi])
                b.op("vector", lambda e, _w=w_ap, _a=a_ap, _mi=mi:
                     e.tensor_single_scalar(
                         _a, _w, thrP[:, 2 * _mi:2 * _mi + 1], ALU.is_gt))
                b.op("vector", lambda e, _w=w_ap, _b=b_ap, _mi=mi:
                     e.tensor_single_scalar(
                         _b, _w, thrP[:, 2 * _mi + 1:2 * _mi + 2],
                         ALU.is_lt))
                gq["qsub"][(mi, g)] = b.op(
                    "vector", lambda e, _a=a_ap, _b=b_ap, _o=o_ap:
                    e.tensor_sub(_o, _a, _b), sem=VE)
                b.wait("gpsimd", VE, gq["qsub"][(mi, g)])
                gq["qstore"][(mi, g)] = b.dma("gpsimd", QS[sl],
                                              wqdst(mi, g), o_ap)
                if g == ntile_w[mi] - 1:
                    gq["qstore_done"][mi] = (QS[0].n, QS[1].n)
            return job

        def mk_chain_job(w, hh, d):
            def job():
                sl = d % 2
                base = sl * 1408
                wqn_t = (w1qn, w2qn)[w]
                wqt_t = (w1qt, w2qt)[w]
                if hh == 0 and d == 0:
                    q0_, q1_ = gq["qstore_done"][w]
                    b.wait("sync", QS[0], q0_)
                    b.wait("sync", QS[1], q1_)
                b.wait("sync", CHs[sl], CHs[sl].n)
                t = b.dma("sync", CHs[sl], tch[:, base:base + 1408],
                          wqn_t[hh * 1408:(hh + 1) * 1408,
                                d * 128:(d + 1) * 128],
                          transpose=True)
                b.wait("sync", CHs[sl], t)
                b.dma("sync", CHs[sl],
                      wqt_t[d * 128:(d + 1) * 128,
                            hh * 1408:(hh + 1) * 1408],
                      tch[:, base:base + 1408])
                if d == 15:
                    gq["chain_w"][w][hh] = (CHs[0].n, CHs[1].n)
            return job

        for mi in range(3):
            for g in range(ntile_w[mi]):
                bg.append(mk_gamma_job(mi, g))
        for g in range(ntile_w[0]):
            bg.append(mk_quant_job(0, g))
        for hh in range(4):
            for d in range(16):
                bg.append(mk_chain_job(0, hh, d))
        for g in range(ntile_w[1]):
            bg.append(mk_quant_job(1, g))
        for hh in range(4):
            for d in range(16):
                bg.append(mk_chain_job(1, hh, d))
        for g in range(ntile_w[2]):
            bg.append(mk_quant_job(2, g))

        b.op("gpsimd", lambda e: e.memset(onec[:], 1.0))
        b.op("gpsimd", lambda e: e.memset(onecf[:], 1.0))
        b.op("gpsimd", lambda e: e.memset(onerf[:], 1.0))
        b.op("gpsimd", lambda e: e.memset(gpart[:], 0.0), sem=PE)
        for i in range(2):
            b.dma("sync", CSB, R2f[:, i * 2048:(i + 1) * 2048],
                  COSB[i * 128:(i + 1) * 128, :])
        for i in range(2):
            b.dma("sync", CSB, R2f[:, 4096 + i * 2048: 4096 + (i + 1) * 2048],
                  SINB[i * 128:(i + 1) * 128, :])
        csb_total = CSB.n

        xt_rel = [0, 0]
        st_store = [0] * 16
        for i in range(16):
            sl = i % 2
            p = sl * 8
            if i >= 2:
                b.wait("sync", VE, xt_rel[sl])
            xl = b.dma("sync", XL[sl], R3f[:, sl * 2048:(sl + 1) * 2048],
                       X[i * 128:(i + 1) * 128, :])
            xt = R3f[:, sl * 2048:(sl + 1) * 2048]
            stg = R1[:, sl * 2048:(sl + 1) * 2048]
            b.wait("vector", XL[sl], xl)
            if i >= 2:
                b.wait("vector", XS[sl], st_store[i - 2])
            b.op("vector", lambda e, _x=xt, _p=p: e.reduce_sum(
                sml[:, _p:_p + 1], _x, axis=AX.X))
            b.op("vector", lambda e, _x=xt, _p=p, _st=stg:
                 e.scalar_tensor_tensor(_st, _x, 1.0, _x, ALU.mult, ALU.mult,
                                        accum_out=sml[:, _p + 1:_p + 2]))
            b.op("vector", lambda e, _p=p: e.tensor_scalar(
                sml[:, _p + 2:_p + 3], sml[:, _p:_p + 1], 1.0 / D, 0.0,
                ALU.mult, ALU.add))
            b.op("vector", lambda e, _p=p: e.tensor_tensor(
                sml[:, _p + 3:_p + 4], sml[:, _p + 2:_p + 3],
                sml[:, _p + 2:_p + 3], ALU.mult))
            b.op("vector", lambda e, _p=p: e.scalar_tensor_tensor(
                sml[:, _p + 4:_p + 5], sml[:, _p + 1:_p + 2], 1.0 / D,
                sml[:, _p + 3:_p + 4], ALU.mult, ALU.subtract))
            vvar = b.op("vector", lambda e, _p=p: e.tensor_scalar(
                sml[:, _p + 4:_p + 5], sml[:, _p + 4:_p + 5], 1.0, 1e-5,
                ALU.mult, ALU.add), sem=VE)
            b.wait("scalar", VE, vvar)
            b.op("scalar", lambda e, _p=p: e.activation(
                sml[:, _p + 5:_p + 6], sml[:, _p + 4:_p + 5], AF.Ln))
            arstd = b.op("scalar", lambda e, _p=p: e.activation(
                sml[:, _p + 6:_p + 7], sml[:, _p + 5:_p + 6], AF.Exp,
                scale=-0.5), sem=AE)
            b.wait("vector", AE, arstd)
            vapp = b.op("vector", lambda e, _x=xt, _p=p, _st=stg:
                        e.tensor_scalar(_st, _x, sml[:, _p + 2:_p + 3],
                                        sml[:, _p + 6:_p + 7],
                                        ALU.subtract, ALU.mult), sem=VE)
            xt_rel[sl] = vapp
            b.wait("sync", VE, vapp)
            st_store[i] = b.dma("sync", XS[sl],
                                xn1s[i * 128:(i + 1) * 128, :], stg)
            pump(2)
        xs_totals = (XS[0].n, XS[1].n)

        b.wait("sync", XS[0], xs_totals[0])
        b.wait("sync", XS[1], xs_totals[1])
        for d in range(16):
            b.dma("sync", XF, R1[:, d * 2048:(d + 1) * 2048],
                  xn1s[:, d * 128:(d + 1) * 128], transpose=True)
        xf_fm = XF.n

        def xn1fm(d, c0, w):
            return R1[:, d * 2048 + c0: d * 2048 + c0 + w]

        wkp_load = [0, 0]
        kgrp_mm = {}
        krope_ve = {}
        kst_tgt = {}
        b.wait("tensor", XF, xf_fm)
        b.wait("vector", CSB, csb_total)
        for j in range(8):
            jsl = j % 2
            if j == 0:
                b.wait("gpsimd", VE, max(xt_rel))
            if j >= 2:
                b.wait("gpsimd", TE, kgrp_mm[(j - 2, 1)])
            wkp_load[jsl] = b.dma(
                "gpsimd", WKs[jsl],
                R3[:, jsl * 4096: jsl * 4096 + 4096].rearrange(
                    "p (c f) -> p c f", f=256),
                r3(Wk)[:, :, 256 * j:256 * (j + 1)])
            for th in range(2):
                par = (2 * j + th) % 2
                c0ps, c1ps = PS[2 * par], PS[2 * par + 1]
                b.wait("tensor", WKs[jsl], wkp_load[jsl])
                prev = (j - 1, th)
                if prev in krope_ve:
                    b.wait("tensor", VE, krope_ve[prev])
                for kc in range(2):
                    for n in range(2):
                        for dd in range(16):
                            b.op("tensor", lambda e, _kc=kc, _n=n, _d=dd,
                                 _jsl=jsl, _th=th,
                                 _ps=(c0ps, c1ps)[kc]: e.matmul(
                                _ps[:, _n * 512:(_n + 1) * 512],
                                R3[:, _jsl * 4096 + _d * 256 + _kc * 128:
                                   _jsl * 4096 + _d * 256 + _kc * 128 + 128],
                                xn1fm(_d, _th * 1024 + _n * 512, 512),
                                start=(_d == 0), stop=(_d == 15)),
                                sem=(TE if (kc == 1 and n == 1 and dd == 15)
                                     else None))
                kgrp_mm[(j, th)] = TE.n
                ksl = (2 * j + th) % 2
                kst = R3[:, 8192 + ksl * 2048: 8192 + (ksl + 1) * 2048]
                b.wait("vector", TE, kgrp_mm[(j, th)])
                prevk = (j - 1, th)
                if prevk in kst_tgt:
                    b.wait("vector", KST[ksl], kst_tgt[prevk])
                cA = R2f[:, th * 1024: th * 1024 + 1024]
                cB = R2f[:, 2048 + th * 1024: 2048 + th * 1024 + 1024]
                sA = R2f[:, 4096 + th * 1024: 4096 + th * 1024 + 1024]
                sB = R2f[:, 6144 + th * 1024: 6144 + th * 1024 + 1024]
                b.op("vector", lambda e, _c=c0ps, _cA=cA: e.tensor_tensor(
                    tmp[:, 0:1024], _c[:, 0:1024], _cA, ALU.mult))
                b.op("vector", lambda e, _c=c1ps, _sA=sA: e.tensor_tensor(
                    tmp[:, 1024:2048], _c[:, 0:1024], _sA, ALU.mult))
                b.op("vector", lambda e, _k=kst: e.tensor_sub(
                    _k[:, 0:1024], tmp[:, 0:1024], tmp[:, 1024:2048]))
                b.op("vector", lambda e, _c=c1ps, _cB=cB: e.tensor_tensor(
                    tmp[:, 0:1024], _c[:, 0:1024], _cB, ALU.mult))
                b.op("vector", lambda e, _c=c0ps, _sB=sB: e.tensor_tensor(
                    tmp[:, 1024:2048], _c[:, 0:1024], _sB, ALU.mult))
                krope_ve[(j, th)] = b.op("vector", lambda e, _k=kst:
                                         e.tensor_add(_k[:, 1024:2048],
                                                      tmp[:, 0:1024],
                                                      tmp[:, 1024:2048]),
                                         sem=VE)
                b.wait("sync", VE, krope_ve[(j, th)])
                t = 0
                for kc in range(2):
                    t = b.dma("sync", KST[ksl],
                              Ks[(2 * j + kc) * 128:(2 * j + kc + 1) * 128,
                                 th * 1024:(th + 1) * 1024],
                              kst[:, kc * 1024:(kc + 1) * 1024])
                kst_tgt[(j, th)] = t
                pump(4)
        kst_totals = (KST[0].n, KST[1].n)
        ve_kropes = VE.n

        def emit_thr(mi):
            b.wait("tensor", AE, gq["pool_red_done"][mi])
            if mi >= 1:
                b.wait("tensor", AE, gq["thr_ae"][mi - 1])
            base = (0, 44, 88)[mi]
            wdt = (44, 44, 64)[mi]
            tcr = b.op("tensor", lambda e, _b=base, _w=wdt: e.matmul(
                PS[3][0:1, 512:512 + _w], onecf[:, 0:1],
                gpart[:, _b:_b + _w], start=True, stop=True), sem=TE)
            b.wait("vector", TE, tcr)
            b.op("vector", lambda e, _mi=mi, _w=wdt: e.reduce_sum(
                gsml[0:1, _mi:_mi + 1], PS[3][0:1, 512:512 + _w],
                axis=AX.X))
            b.op("vector", lambda e, _mi=mi: e.tensor_scalar(
                gsml[0:1, 3 + _mi:4 + _mi], gsml[0:1, _mi:_mi + 1],
                0.5 / NW, 0.5e-8, ALU.mult, ALU.add))
            vthr = b.op("vector", lambda e, _mi=mi: e.tensor_scalar(
                gsml[0:1, 7:8], gsml[0:1, 3 + _mi:4 + _mi], -1.0, 0.0,
                ALU.mult, ALU.add), sem=VE)
            b.wait("tensor", VE, vthr)
            b.op("tensor", lambda e, _mi=mi: e.matmul(
                PS[3][:, 512:513], onerf[0:1, :], gsml[0:1, 3 + _mi:4 + _mi],
                start=True, stop=True))
            tbc = b.op("tensor", lambda e: e.matmul(
                PS[3][:, 513:514], onerf[0:1, :], gsml[0:1, 7:8],
                start=True, stop=True), sem=TE)
            b.wait("scalar", TE, tbc)
            gq["thr_ae"][mi] = b.op("scalar", lambda e, _mi=mi: e.activation(
                thrP[:, 2 * _mi:2 * _mi + 2], PS[3][:, 512:514], AF.Copy),
                sem=AE)

        b.wait("tensor", VE, ve_kropes)
        pump(max(0, 88 - bg_i[0]))
        emit_thr(0)
        emit_thr(1)

        wvq_load = [0, 0]
        v_mm = {}
        v_evac = {}
        vst_tgt = {}
        for fq in range(4):
            fsl = fq % 2
            if fq == 0:
                b.wait("gpsimd", TE, kgrp_mm[(7, 1)])
            if fq == 1:
                b.wait("gpsimd", KST[0], kst_totals[0])
                b.wait("gpsimd", KST[1], kst_totals[1])
            if fq >= 2:
                b.wait("gpsimd", TE, v_mm[(fq - 2, 15)])
            wvq_load[fsl] = b.dma(
                "gpsimd", WVs[fsl],
                R3[:, fsl * 8192: fsl * 8192 + 8192].rearrange(
                    "p (c f) -> p c f", f=512),
                r3(Wv)[:, :, 512 * fq:512 * (fq + 1)])
            b.wait("tensor", WVs[fsl], wvq_load[fsl])
            for tt in range(16):
                vpar = tt % 2
                prev = (fq, tt - 2) if tt >= 2 else \
                    ((fq - 1, 14 + tt) if fq else None)
                if prev in v_evac:
                    b.wait("tensor", AE, v_evac[prev])
                for dd in range(16):
                    b.op("tensor", lambda e, _d=dd, _tt=tt, _fsl=fsl,
                         _vp=vpar: e.matmul(
                        ps(0, _vp),
                        xn1fm(_d, _tt * 128, 128),
                        R3[:, _fsl * 8192 + _d * 512:
                           _fsl * 8192 + (_d + 1) * 512],
                        start=(_d == 0), stop=(_d == 15)),
                        sem=(TE if dd == 15 else None))
                v_mm[(fq, tt)] = TE.n
                b.wait("scalar", TE, v_mm[(fq, tt)])
                gi = fq * 4 + (tt // 4)
                if tt % 4 == 0:
                    pv = vst_tgt.get(gi - 2)
                    if pv:
                        b.wait("scalar", VST[gi % 2], pv)
                gh = (gi % 2) * 2048
                v_evac[(fq, tt)] = b.op(
                    "scalar", lambda e, _tt=tt, _vp=vpar, _gh=gh:
                    e.activation(
                        qp[:, _gh + (_tt % 4) * 512:
                           _gh + ((_tt % 4) + 1) * 512],
                        ps(0, _vp), AF.Copy), sem=AE)
                if tt % 4 == 3:
                    grp = tt // 4
                    b.wait("sync", AE, v_evac[(fq, tt)])
                    vst_tgt[gi] = b.dma(
                        "sync", VST[gi % 2],
                        Vs2[fq * 2048 + grp * 512:
                            fq * 2048 + (grp + 1) * 512,
                            :].rearrange("(t p) f -> p t f", p=128),
                        qp[:, gh:gh + 2048].rearrange(
                            "p (t f) -> p t f", f=512))
                pump(2)
        vst_totals = (VST[0].n, VST[1].n)
        te_vdone = TE.n
        pump(max(0, 152 - bg_i[0]))
        b.wait("tensor", AE, gq["thr_ae"][1])
        emit_thr(2)

        b.wait("sync", VE, ve_kropes)
        b.dma("sync", CSA, R2[:, 0:8192].rearrange("p (m f) -> p m f", f=512),
              MSK.rearrange("(m p) f -> p m f", p=128))
        for i, (srct, off) in enumerate([(COSB, 4096), (COSB, 5120),
                                         (SINB, 6144), (SINB, 7168)]):
            half = i % 2
            b.dma("sync", CSA, R2f[:, off:off + 1024],
                  srct[half * 128:(half + 1) * 128, 0:1024])
        csa_total = CSA.n

        def y_blk(h, qhl, vc):
            blk = 4 * h + 2 * qhl + vc
            return R1[:, blk * 512:(blk + 1) * 512]

        kh_load = {}
        vh_load = {}
        wqp_load = {}
        qbuild_te = {}
        qrope_ve = {}
        ev_pv = {}
        ev_tanh = {}
        ev_exp = {}
        ev_mask = {}
        ynorm_ve = {}
        wo_stop = {}
        wo_evac = {}
        x2store = {}
        xn2store = {}
        first_vec_att = [True]
        last_ynorm = [0]
        ve_after_ln2 = [0, 0]
        bcast_ae = {}

        for s in range(2):
            EV = EA if s == 0 else EB
            nev = len(EV)
            if s == 0:
                b.wait("sync", TE, te_vdone)
            if s == 1:
                b.wait("sync", TE, qbuild_te[(0, 7)])
                b.wait("sync", X2S, x2store[0])
                b.wait("sync", VE, ve_after_ln2[0])
            for d in range(16):
                b.dma("sync", XF,
                      R1[:, 16384 + d * 512: 16384 + (d + 1) * 512],
                      xn1s[s * 512:(s + 1) * 512, d * 128:(d + 1) * 128],
                      transpose=True)
            xf_sl = XF.n

            for h in range(8):
                hsl = h % 2
                if (s, h) == (0, 0):
                    b.wait("sync", KST[0], kst_totals[0])
                    b.wait("sync", KST[1], kst_totals[1])
                    b.wait("sync", VST[0], vst_totals[0])
                    b.wait("sync", VST[1], vst_totals[1])
                rel = ev_pv.get((s, h - 2, nev - 1)) or \
                    (ev_pv.get((0, 6 + h, 7)) if s == 1 else None)
                if rel:
                    b.wait("sync", TE, rel)
                if (s, h) in [(0, 0), (0, 1)]:
                    b.wait("sync", TE, te_vdone)
                if (s, h) in [(1, 0), (1, 1)]:
                    b.wait("sync", TE, wo_stop[(0, 3)])
                kh_load[(s, h)] = b.dma(
                    "sync", KH[hsl],
                    R3[:, 8192 + hsl * 4096:
                       8192 + (hsl + 1) * 4096].rearrange(
                        "p (c f) -> p c f", f=2048),
                    r3(Ks)[:, 2 * h:2 * h + 2, :])
                vh_load[(s, h)] = b.dma(
                    "sync", VH[hsl],
                    R1[:, 24576 + hsl * 4096:
                       24576 + (hsl + 1) * 4096].rearrange(
                        "p (c f) -> p c f", f=256),
                    r3(Vs2[(h // 2) * 2048:(h // 2 + 1) * 2048,
                           (h % 2) * 256:(h % 2) * 256 + 256]))
                relq = qbuild_te.get((s, h - 1)) or \
                    (wo_stop.get((0, 3)) if s == 1 and h == 0 else None)
                if relq:
                    b.wait("gpsimd", TE, relq)
                elif (s, h) == (0, 0):
                    b.wait("gpsimd", TE, te_vdone)
                wqp_load[(s, h)] = b.dma(
                    "gpsimd", WQP,
                    R3[:, 0:8192].rearrange("p (c f) -> p c f", f=512),
                    r3(Wq)[:, :, 512 * h:512 * (h + 1)])

                b.wait("tensor", WQP, wqp_load[(s, h)])
                b.wait("tensor", XF, xf_sl)
                if (s, h) == (0, 0):
                    b.wait("tensor", CSA, csa_total)
                relb = ynorm_ve.get((s, h - 1)) or \
                    (ynorm_ve.get((0, 7)) if s == 1 and h == 0 else None)
                if relb:
                    b.wait("tensor", VE, relb)
                for pr in range(2):
                    for qc in range(2):
                        for dd in range(16):
                            b.op("tensor", lambda e, _pr=pr, _qc=qc,
                                 _d=dd: e.matmul(
                                ps(1 + _pr, _qc),
                                R3[:, _d * 512 + (_pr * 2 + _qc) * 128:
                                   _d * 512 + (_pr * 2 + _qc) * 128 + 128],
                                R1[:, 16384 + _d * 512:
                                   16384 + (_d + 1) * 512],
                                start=(_d == 0), stop=(_d == 15)),
                                sem=(TE if (qc == 1 and dd == 15) else None))
                    qb = TE.n
                    if pr == 1:
                        qbuild_te[(s, h)] = qb
                    if first_vec_att[0]:
                        b.wait("vector", CSA, csa_total)
                        b.wait("vector", VST[0], vst_totals[0])
                        b.wait("vector", VST[1], vst_totals[1])
                        first_vec_att[0] = False
                    if h == 0 and pr == 0:
                        for _xi in range(4):
                            b.wait("vector", XR[_xi], XR[_xi].n)
                    b.wait("vector", TE, qb)
                    qo = hsl * 2048 + pr * 1024
                    sfx = s * 512
                    cA = R2f[:, 4096 + sfx: 4096 + sfx + 512]
                    cB = R2f[:, 5120 + sfx: 5120 + sfx + 512]
                    sA = R2f[:, 6144 + sfx: 6144 + sfx + 512]
                    sB = R2f[:, 7168 + sfx: 7168 + sfx + 512]
                    b.op("vector", lambda e, _pr=pr, _cA=cA: e.tensor_tensor(
                        xres[:, 0:512], ps(1 + _pr, 0), _cA, ALU.mult))
                    b.op("vector", lambda e, _pr=pr, _sA=sA: e.tensor_tensor(
                        xres[:, 512:1024], ps(1 + _pr, 1), _sA, ALU.mult))
                    b.op("vector", lambda e, _qo=qo: e.tensor_sub(
                        qp[:, _qo:_qo + 512], xres[:, 0:512],
                        xres[:, 512:1024]))
                    b.op("vector", lambda e, _pr=pr, _cB=cB: e.tensor_tensor(
                        xres[:, 0:512], ps(1 + _pr, 1), _cB, ALU.mult))
                    b.op("vector", lambda e, _pr=pr, _sB=sB: e.tensor_tensor(
                        xres[:, 512:1024], ps(1 + _pr, 0), _sB, ALU.mult))
                    qrope_ve[(s, h, pr)] = b.op(
                        "vector", lambda e, _qo=qo: e.tensor_add(
                            qp[:, _qo + 512:_qo + 1024],
                            xres[:, 0:512], xres[:, 512:1024]), sem=VE)
                pump(2)

                b.wait("tensor", VE, qrope_ve[(s, h, 1)])
                b.wait("tensor", KH[hsl], kh_load[(s, h)])
                b.wait("tensor", VH[hsl], vh_load[(s, h)])
                qk_te = {}

                def emit_qk(ei, s=s, h=h, hsl=hsl, EV=EV):
                    stile = EV[ei]
                    if ei >= 1:
                        b.wait("tensor", AE, ev_tanh[(s, h, ei - 1)])
                    elif (s, h) != (0, 0):
                        ph = (s, h - 1) if h else (0, 7)
                        b.wait("tensor", AE, bcast_ae[ph])
                    for kc in range(2):
                        for qhl in range(2):
                            b.op("tensor", lambda e, _kc=kc, _q=qhl,
                                 _st=stile, _hsl=hsl: e.matmul(
                                ps(0, _q),
                                R3[:, 8192 + _hsl * 4096 + _kc * 2048 +
                                   _st * 128: 8192 + _hsl * 4096 +
                                   _kc * 2048 + _st * 128 + 128],
                                qp[:, _hsl * 2048 + _q * 1024 + _kc * 512:
                                   _hsl * 2048 + _q * 1024 +
                                   (_kc + 1) * 512],
                                start=(_kc == 0), stop=(_kc == 1)),
                                sem=(TE if (kc == 1 and qhl == 1) else None))
                    qk_te[ei] = TE.n

                def emit_softcap(ei, s=s, h=h, nev=nev):
                    par = ei % 2
                    b.wait("scalar", TE, qk_te[ei])
                    b.op("scalar", lambda e, _par=par: e.activation(
                        tmp[:, _par * 1024: _par * 1024 + 512], ps(0, 0),
                        AF.Tanh, scale=0.00125))
                    ev_tanh[(s, h, ei)] = b.op(
                        "scalar", lambda e, _par=par: e.activation(
                            tmp[:, _par * 1024 + 512: _par * 1024 + 1024],
                            ps(0, 1), AF.Tanh, scale=0.00125), sem=AE)
                    if ei >= 2:
                        b.wait("scalar", TE, ev_pv[(s, h, ei - 2)])
                    elif (s, h) != (0, 0):
                        ph = (s, h - 1) if h else (0, 7)
                        pe_ = (7 if ph[0] == 0 else 15) - (1 - ei)
                        b.wait("scalar", TE, ev_pv[(ph[0], ph[1], pe_)])
                    if (s, h, ei) == (1, 0, 0):
                        b.wait("scalar", XN2, xn2store[(0, 3)])
                    b.op("scalar", lambda e, _par=par: e.activation(
                        pbuf[:, _par * 1024: _par * 1024 + 512],
                        tmp[:, _par * 1024: _par * 1024 + 512],
                        AF.Exp, scale=50.0))
                    ev_exp[(s, h, ei)] = b.op(
                        "scalar", lambda e, _par=par: e.activation(
                            pbuf[:, _par * 1024 + 512: _par * 1024 + 1024],
                            tmp[:, _par * 1024 + 512: _par * 1024 + 1024],
                            AF.Exp, scale=50.0), sem=AE)

                def emit_mask(ei, s=s, h=h, EV=EV):
                    stile = EV[ei]
                    par = ei % 2
                    masked = (s == 0) or (stile in MASKED_B)
                    if not masked:
                        return
                    midx = ei if s == 0 else \
                        (8 + [4, 5, 6, 7, 12, 14, 15].index(stile)
                         if False else
                         8 + [4, 5, 6, 7, 12, 13, 14, 15].index(stile))
                    b.wait("vector", AE, ev_exp[(s, h, ei)])
                    b.op("vector", lambda e, _par=par, _m=midx:
                         e.tensor_mul(
                             pbuf[:, _par * 1024: _par * 1024 + 512],
                             pbuf[:, _par * 1024: _par * 1024 + 512],
                             R2[:, _m * 512:(_m + 1) * 512]))
                    ev_mask[(s, h, ei)] = b.op(
                        "vector", lambda e, _par=par, _m=midx:
                        e.tensor_mul(
                            pbuf[:, _par * 1024 + 512: _par * 1024 + 1024],
                            pbuf[:, _par * 1024 + 512: _par * 1024 + 1024],
                            R2[:, _m * 512:(_m + 1) * 512]), sem=VE)

                def emit_pv(ei, s=s, h=h, hsl=hsl, EV=EV, nev=nev):
                    stile = EV[ei]
                    par = ei % 2
                    masked = (s == 0) or (stile in MASKED_B)
                    if masked:
                        b.wait("tensor", VE, ev_mask[(s, h, ei)])
                    else:
                        b.wait("tensor", AE, ev_exp[(s, h, ei)])
                    first = (ei == 0)
                    lastE = (ei == nev - 1)
                    for vc in range(2):
                        for qhl in range(2):
                            b.op("tensor", lambda e, _vc=vc, _q=qhl,
                                 _st=stile, _hsl=hsl, _par=par, _f=first,
                                 _l=lastE: e.matmul(
                                ps(1 + _q, _vc),
                                R1[:, 24576 + _hsl * 4096 + _st * 256 +
                                   _vc * 128: 24576 + _hsl * 4096 +
                                   _st * 256 + _vc * 128 + 128],
                                pbuf[:, _par * 1024 + _q * 512:
                                     _par * 1024 + (_q + 1) * 512],
                                start=_f, stop=_l))
                    b.op("tensor", lambda e, _par=par, _f=first, _l=lastE:
                         e.matmul(PS[3][0:1, 0:512], onec[:, 0:1],
                                  pbuf[:, _par * 1024: _par * 1024 + 512],
                                  start=_f, stop=_l))
                    ev_pv[(s, h, ei)] = b.op(
                        "tensor", lambda e, _par=par, _f=first, _l=lastE:
                        e.matmul(PS[3][0:1, 512:1024], onec[:, 0:1],
                                 pbuf[:, _par * 1024 + 512:
                                      _par * 1024 + 1024],
                                 start=_f, stop=_l), sem=TE)

                emit_qk(0)
                emit_softcap(0)
                emit_mask(0)
                for ei in range(nev):
                    if ei + 1 < nev:
                        emit_qk(ei + 1)
                        emit_softcap(ei + 1)
                        emit_mask(ei + 1)
                    emit_pv(ei)
                    if ei % 2 == 1:
                        pump(1)

                b.wait("vector", TE, ev_pv[(s, h, nev - 1)])
                b.op("vector", lambda e: e.reciprocal(
                    tmp[0:1, 1024:1536], PS[3][0:1, 0:512]))
                vr2 = b.op("vector", lambda e: e.reciprocal(
                    tmp[0:1, 1536:2048], PS[3][0:1, 512:1024]), sem=VE)
                b.wait("tensor", VE, vr2)
                b.wait("tensor", AE, ev_tanh[(s, h, nev - 1)])
                b.op("tensor", lambda e: e.matmul(
                    PS[3][:, 512:1024], onerf[0:1, :], tmp[0:1, 1024:1536],
                    start=True, stop=True))
                tb1 = b.op("tensor", lambda e: e.matmul(
                    ps(0, 0), onerf[0:1, :], tmp[0:1, 1536:2048],
                    start=True, stop=True), sem=TE)
                b.wait("scalar", TE, tb1)
                b.wait("scalar", VE, last_ynorm[0])
                b.op("scalar", lambda e: e.activation(
                    bcs[:, 0:512], PS[3][:, 512:1024], AF.Copy))
                ta1 = b.op("scalar", lambda e: e.activation(
                    bcs[:, 512:1024], ps(0, 0), AF.Copy), sem=AE)
                bcast_ae[(s, h)] = ta1
                b.wait("vector", AE, ta1)
                for qhl in range(2):
                    for vc in range(2):
                        b.op("vector", lambda e, _q=qhl, _vc=vc, _h=h:
                             e.tensor_mul(y_blk(_h, _q, _vc),
                                          ps(1 + _q, _vc),
                                          bcs[:, _q * 512:(_q + 1) * 512]),
                             sem=VE)
                ynorm_ve[(s, h)] = VE.n
                last_ynorm[0] = VE.n
                pump(4)

            wo_load = {}
            for nq in range(4):
                for hf in range(2):
                    if nq == 0 and hf == 0:
                        b.wait("gpsimd", TE, ev_pv[(s, 7, nev - 1)])
                    elif nq and hf == 0:
                        b.wait("gpsimd", TE, wo_stop[(s, nq - 1)])
                    wo_load[(nq, hf)] = b.dma(
                        "gpsimd", WOs[hf],
                        R3[:, hf * 8192: hf * 8192 + 8192].rearrange(
                            "p (c f) -> p c f", f=512),
                        r3(Wo[hf * 2048:(hf + 1) * 2048,
                              nq * 512:(nq + 1) * 512]))
                wpar = nq % 2
                b.wait("tensor", VE, ynorm_ve[(s, 7)])
                if nq >= 2:
                    for tt in range(4):
                        b.wait("tensor", VE, wo_evac[(s, nq - 2, tt)])
                for kc in range(32):
                    hf = kc // 16
                    if kc % 16 == 0:
                        b.wait("tensor", WOs[hf], wo_load[(nq, hf)])
                    for tt in range(4):
                        b.op("tensor", lambda e, _kc=kc, _tt=tt, _hf=hf,
                             _wp=wpar: e.matmul(
                            ps(2 * _wp + _tt // 2, _tt % 2),
                            R1[:, _kc * 512 + _tt * 128:
                               _kc * 512 + _tt * 128 + 128],
                            R3[:, _hf * 8192 + (_kc % 16) * 512:
                               _hf * 8192 + (_kc % 16) * 512 + 512],
                            start=(_kc == 0), stop=(_kc == 31)),
                            sem=(TE if (kc == 31 and tt == 3) else None))
                wo_stop[(s, nq)] = TE.n
                for tt in range(4):
                    if nq == 0 and tt == 0:
                        b.wait("sync", VE, qrope_ve[(s, 7, 1)])
                    pv = wo_evac.get((s, nq - 1, tt)) if nq else None
                    if pv:
                        b.wait("sync", VE, pv)
                    xr = b.dma("sync", XR[tt],
                               xres[:, tt * 512:(tt + 1) * 512],
                               X[s * 512 + tt * 128:
                                 s * 512 + (tt + 1) * 128,
                                 nq * 512:(nq + 1) * 512])
                    b.wait("vector", XR[tt], xr)
                    if s == 1 and nq == 0 and tt == 0:
                        b.wait("vector", X2S, x2store[0])
                    b.wait("vector", TE, wo_stop[(s, nq)])
                    wo_evac[(s, nq, tt)] = b.op(
                        "vector", lambda e, _tt=tt, _wp=wpar, _nq=nq:
                        e.tensor_add(
                            R1f[:, 8192 + _tt * 2048 + _nq * 512:
                                8192 + _tt * 2048 + (_nq + 1) * 512],
                            ps(2 * _wp + _tt // 2, _tt % 2),
                            xres[:, _tt * 512:(_tt + 1) * 512]),
                        sem=VE)
                pump(2)

            for tt in range(4):
                for nq in range(4):
                    b.wait("vector", VE, wo_evac[(s, nq, tt)])
                xv = R1f[:, 8192 + tt * 2048: 8192 + (tt + 1) * 2048]
                b.op("vector", lambda e, _x=xv, _tt=tt: e.reduce_sum(
                    ln2m[:, _tt:_tt + 1], _x, axis=AX.X))
                b.op("vector", lambda e, _x=xv, _tt=tt:
                     e.scalar_tensor_tensor(
                         tmp[:, 0:2048], _x, 1.0, _x, ALU.mult, ALU.mult,
                         accum_out=ln2m[:, 4 + _tt:5 + _tt]))
                b.op("vector", lambda e, _tt=tt: e.tensor_scalar(
                    ln2m[:, _tt:_tt + 1], ln2m[:, _tt:_tt + 1], 1.0 / D,
                    0.0, ALU.mult, ALU.add))
                b.op("vector", lambda e, _tt=tt: e.tensor_tensor(
                    ln2m[:, 8 + _tt:9 + _tt], ln2m[:, _tt:_tt + 1],
                    ln2m[:, _tt:_tt + 1], ALU.mult))
                b.op("vector", lambda e, _tt=tt: e.scalar_tensor_tensor(
                    ln2m[:, 4 + _tt:5 + _tt], ln2m[:, 4 + _tt:5 + _tt],
                    1.0 / D, ln2m[:, 8 + _tt:9 + _tt], ALU.mult,
                    ALU.subtract))
                v2 = b.op("vector", lambda e, _tt=tt: e.tensor_scalar(
                    ln2m[:, 4 + _tt:5 + _tt], ln2m[:, 4 + _tt:5 + _tt],
                    1.0, 1e-5, ALU.mult, ALU.add), sem=VE)
                b.wait("scalar", VE, v2)
                b.op("scalar", lambda e, _tt=tt: e.activation(
                    ln2m[:, 4 + _tt:5 + _tt], ln2m[:, 4 + _tt:5 + _tt],
                    AF.Ln))
                a2 = b.op("scalar", lambda e, _tt=tt: e.activation(
                    ln2m[:, 8 + _tt:9 + _tt], ln2m[:, 4 + _tt:5 + _tt],
                    AF.Exp, scale=-0.5), sem=AE)
                b.wait("vector", AE, a2)
                if (s, tt) != (0, 0):
                    prevst = xn2store.get((s, tt - 1)) or xn2store.get((0, 3))
                    if prevst:
                        b.wait("vector", XN2, prevst)
                ap2 = b.op("vector", lambda e, _x=xv, _tt=tt:
                           e.tensor_scalar(
                               pbuf[:, 0:2048], _x, ln2m[:, _tt:_tt + 1],
                               ln2m[:, 8 + _tt:9 + _tt], ALU.subtract,
                               ALU.mult), sem=VE)
                b.wait("sync", VE, ap2)
                xn2store[(s, tt)] = b.dma(
                    "sync", XN2,
                    xn2s[s * 512 + tt * 128: s * 512 + (tt + 1) * 128, :],
                    pbuf[:, 0:2048])
                pump(1)
            ve_after_ln2[s] = VE.n
            b.wait("sync", VE, wo_evac[(s, 3, 3)])
            x2store[s] = b.dma(
                "sync", X2S,
                x2s[s * 512:(s + 1) * 512, :].rearrange(
                    "(t p) f -> p t f", p=128),
                R1f[:, 8192:16384].rearrange("p (t f) -> p t f", f=2048))

        pump_all()
        b.wait("sync", VE, gq["qsub"][(2, ntile_w[2] - 1)])
        q0, q1 = gq["qstore_done"][2]
        b.wait("sync", QS[0], q0)
        b.wait("sync", QS[1], q1)
        b.wait("sync", XN2, xn2store[(1, 3)])
        for d in range(16):
            b.dma("sync", XF, R4[:, d * 1024:(d + 1) * 1024],
                  xn2s[:, d * 128:(d + 1) * 128], transpose=True)
        xf_xn2 = XF.n

        def hid(hc, c0, w):
            if hc < 32:
                return R1[:, hc * 1024 + c0: hc * 1024 + c0 + w]
            return R2[:, (hc - 32) * 1024 + c0: (hc - 32) * 1024 + c0 + w]

        use_tgt = {}
        use_cons = {}
        sig_ae = {}
        hmul_ve = {}
        for hc in range(44):
            par = hc % 2
            hh = min(hc // 11, 3)
            for m in range(2):
                c0_, c1_ = gq["chain_w"][m][hh]
                b.wait("sync", CHs[0], c0_)
                b.wait("sync", CHs[1], c1_)
                prev = use_cons.get(hc - 2)
                if prev:
                    b.wait("sync", TE, prev)
                wqt_t = (w1qt, w2qt)[m]
                use_tgt[(hc, m)] = b.dma(
                    "sync", USE[par],
                    R3[:, m * 4096 + par * 2048:
                       m * 4096 + (par + 1) * 2048].rearrange(
                        "p (c f) -> p c f", f=128),
                    r3(wqt_t)[:, :, hc * 128:(hc + 1) * 128])
            b.wait("tensor", USE[par], use_tgt[(hc, 1)])
            if hc == 0:
                b.wait("tensor", XF, xf_xn2)
            if hc >= 2:
                b.wait("tensor", AE, sig_ae[hc - 2])
                b.wait("tensor", VE, hmul_ve[hc - 2])
            for m in range(2):
                for n in range(2):
                    for dd in range(16):
                        b.op("tensor", lambda e, _m=m, _n=n, _d=dd,
                             _p=par: e.matmul(
                            ps(2 * _p + _m, _n),
                            R3[:, _m * 4096 + _p * 2048 + _d * 128:
                               _m * 4096 + _p * 2048 + (_d + 1) * 128],
                            R4[:, _d * 1024 + _n * 512:
                               _d * 1024 + (_n + 1) * 512],
                            start=(_d == 0), stop=(_d == 15)),
                            sem=(TE if (m == 1 and n == 1 and dd == 15)
                                 else None))
            xte = TE.n
            use_cons[hc] = xte
            b.wait("scalar", TE, xte)
            if hc >= 2:
                b.wait("scalar", VE, hmul_ve[hc - 2])
            for n in range(2):
                sig_ae[hc] = b.op("scalar", lambda e, _n=n, _p=par:
                                  e.activation(
                                      sgt[:, _p * 1024 + _n * 512:
                                          _p * 1024 + (_n + 1) * 512],
                                      ps(2 * _p, _n), AF.Sigmoid), sem=AE)
            b.wait("vector", AE, sig_ae[hc])
            if hc == 0:
                b.wait("vector", X2S, x2store[1])
            for n in range(2):
                hmul_ve[hc] = b.op("vector", lambda e, _n=n, _p=par, _hc=hc:
                                   e.tensor_mul(
                                       hid(_hc, _n * 512, 512),
                                       ps(2 * _p + 1, _n),
                                       sgt[:, _p * 1024 + _n * 512:
                                           _p * 1024 + (_n + 1) * 512]),
                                   sem=VE)
        ve_hidden = VE.n

        w3_tgt = {}
        w3_cons = {}
        w3_evac = {}
        out_store = {}
        for nq in range(4):
            for hc in range(44):
                w3par = hc % 2
                if (nq, hc) == (0, 0):
                    b.wait("sync", QS[0], q0)
                    b.wait("sync", QS[1], q1)
                prev = w3_cons.get((nq, hc - 2)) or \
                    (w3_cons.get((nq - 1, 42 + hc)) if nq else None)
                if prev:
                    b.wait("sync", TE, prev)
                w3_tgt[(nq, hc)] = b.dma(
                    "sync", W3R[w3par],
                    w3rt[:, w3par * 512:(w3par + 1) * 512],
                    w3qn[nq * 512:(nq + 1) * 512, hc * 128:(hc + 1) * 128],
                    transpose=True)
                b.wait("tensor", W3R[w3par], w3_tgt[(nq, hc)])
                if hc == 0:
                    b.wait("tensor", VE, ve_hidden)
                    if nq >= 1:
                        for tt in range(8):
                            b.wait("tensor", VE, w3_evac[(nq - 1, tt)])
                for tt in range(8):
                    b.op("tensor", lambda e, _tt=tt, _hc=hc, _p=w3par:
                         e.matmul(
                             ps(_tt // 2, _tt % 2),
                             hid(_hc, _tt * 128, 128),
                             w3rt[:, _p * 512:(_p + 1) * 512],
                             start=(_hc == 0), stop=(_hc == 43)),
                         sem=(TE if tt == 7 else None))
                w3_cons[(nq, hc)] = TE.n
            w3te = TE.n
            for tt in range(8):
                pv = w3_evac.get((nq - 1, tt)) if nq else None
                if pv:
                    b.wait("sync", VE, pv)
                if nq == 0 and tt == 0:
                    b.wait("sync", X2S, x2store[1])
                    b.wait("sync", VE, wo_evac[(1, 3, 3)])
                    for _xi in range(8):
                        b.wait("sync", XR[_xi], XR[_xi].n)
                if tt >= 4:
                    b.wait("sync", VE, w3_evac[(nq, tt - 4)])
                xr = b.dma("sync", XR[tt],
                           xres[:, (tt % 4) * 512:((tt % 4) + 1) * 512],
                           x2s[tt * 128:(tt + 1) * 128,
                               nq * 512:(nq + 1) * 512])
                b.wait("vector", XR[tt], xr)
                b.wait("vector", TE, w3te)
                if nq and tt == 0:
                    b.wait("vector", OUT, out_store[nq - 1])
                w3_evac[(nq, tt)] = b.op(
                    "vector", lambda e, _tt=tt: e.tensor_add(
                        R3f[:, 4096 + _tt * 512: 4096 + (_tt + 1) * 512],
                        ps(_tt // 2, _tt % 2),
                        xres[:, (_tt % 4) * 512:((_tt % 4) + 1) * 512]),
                    sem=VE)
            b.wait("sync", VE, w3_evac[(nq, 7)])
            out_store[nq] = b.dma(
                "sync", OUT,
                Y.rearrange("(t p) f -> p t f",
                            p=128)[:, :, nq * 512:(nq + 1) * 512],
                R3f[:, 4096:8192].rearrange("p (t f) -> p t f", f=512))
        b.wait("sync", OUT, out_store[3])

        assert bg_i[0] == len(bg), f"unpumped bg jobs: {bg_i[0]}/{len(bg)}"

        @block.tensor
        def _(eng):
            for f in b.q["tensor"]:
                f(eng)

        @block.vector
        def _(eng):
            for f in b.q["vector"]:
                f(eng)

        @block.scalar
        def _(eng):
            for f in b.q["scalar"]:
                f(eng)

        @block.gpsimd
        def _(eng):
            for f in b.q["gpsimd"]:
                f(eng)

        @block.sync
        def _(eng):
            for f in b.q["sync"]:
                f(eng)

    return nc


_NC_CACHE = [None]


def _host_tables():
    inv = 1.0 / (10000.0 ** (np.arange(0, 256, 2, dtype=np.float64) / 256.0))
    return inv


def kernel(X=None, **kw):
    if X is None:
        X = kw.pop("L0_attn_X")
    Wq = kw.pop("L0_attn_Wq")
    Wk = kw.pop("L0_attn_Wk")
    Wv = kw.pop("L0_attn_Wv")
    Wo = kw.pop("L0_attn_Wo")
    W1 = kw.pop("L0_ffn_w1")
    W2 = kw.pop("L0_ffn_w2")
    W3 = kw.pop("L0_ffn_w3")
    X = np.asarray(X, dtype=np.float32)
    B = X.shape[0]

    inv = _host_tables()
    pos_all = np.arange(TB, dtype=np.float64)

    tri = np.zeros((4, 128, 512), dtype=np.float32)
    for j in range(4):
        tri[j] = (np.arange(512)[None, :] >=
                  (np.arange(128)[:, None] + 128 * j))
    ones_t = np.ones((128, 512), np.float32)
    zeros_t = np.zeros((128, 512), np.float32)

    if _NC_CACHE[0] is None:
        _NC_CACHE[0] = build_program()
    nc = _NC_CACHE[0]

    in_maps = []
    metas = []
    for core in range(8):
        bb, sg_ = divmod(core, 2)
        order = [0, 3, 1, 2] if sg_ == 0 else [1, 2, 0, 3]
        rows = np.concatenate([np.arange(c * 512, (c + 1) * 512)
                               for c in order])
        Xp = np.ascontiguousarray(X[bb][rows])
        posg = pos_all[rows]
        ang = inv[:, None] * posg[None, :]
        cosb = np.concatenate([np.cos(ang), np.cos(ang)], 0).astype(np.float32)
        sinb = np.concatenate([np.sin(ang), np.sin(ang)], 0).astype(np.float32)
        mtiles = [tri[j] for j in range(4)]
        mtiles += [zeros_t if sg_ == 0 else ones_t] * 4
        mtiles += [tri[j] for j in range(4)]
        mtiles += [ones_t if sg_ == 0 else zeros_t] * 4
        msk = np.concatenate(mtiles, 0).astype(ml_dtypes.bfloat16)
        in_maps.append({
            "X": Xp, "Wq": np.asarray(Wq, np.float32),
            "Wk": np.asarray(Wk, np.float32),
            "Wv": np.asarray(Wv, np.float32),
            "Wo": np.asarray(Wo, np.float32),
            "W1": np.asarray(W1, np.float32),
            "W2": np.asarray(W2, np.float32),
            "W3": np.asarray(W3, np.float32),
            "COSB": cosb, "SINB": sinb, "MSK": msk,
        })
        metas.append((bb, order))

    res = run_bass_kernel_spmd(nc, in_maps, core_ids=list(range(8)))

    out = np.empty((B, TB, D), dtype=np.float32)
    for core in range(8):
        bb, order = metas[core]
        Yc = np.asarray(res.results[core]["Y"])
        cA, cB = order[0], order[1]
        out[bb, cA * 512:(cA + 1) * 512] = Yc[0:512]
        out[bb, cB * 512:(cB + 1) * 512] = Yc[512:1024]
    return out


# version 04
# speedup vs baseline: 2.3665x; 1.0062x over previous
import contextlib
import numpy as np
import ml_dtypes
import concourse.bass as bass
import concourse.mybir as mybir
from concourse.bass_utils import run_bass_kernel_spmd

F32 = mybir.dt.float32
BF16 = mybir.dt.bfloat16
AF = mybir.ActivationFunctionType
ALU = mybir.AluOpType
AX = mybir.AxisListType

D = 2048
TB = 2048
OWN = 1024
HFF = 5632
NW = float(HFF * D)
EA = [0, 1, 2, 3, 8, 9, 10, 11]
EB = list(range(16))
MASKED_B = {4, 5, 6, 7, 12, 13, 14, 15}


class Sem:
    def __init__(self, h):
        self.h = h
        self.n = 0


class Build:
    ENG = ("tensor", "vector", "scalar", "gpsimd", "sync")

    def __init__(self):
        self.q = {e: [] for e in self.ENG}
        self.auto = {}

    def emit(self, eng, fn):
        self.q[eng].append(fn)

    def wait(self, eng, sem, val):
        if val > 0:
            self.emit(eng, lambda e, _s=sem.h, _v=val: e.wait_ge(_s, _v))

    def dma(self, qeng, sem, dst, src, transpose=False):
        sem.n += 16
        tgt = sem.n

        def f(e, _d=dst, _s=src, _t=transpose, _h=sem.h):
            if _t:
                e.dma_start_transpose(_d, _s).then_inc(_h, 16)
            else:
                e.dma_start(_d, _s).then_inc(_h, 16)

        self.emit(qeng, f)
        return tgt

    def op(self, eng, fn, sem=None, k=1):
        a = self.auto.get(eng)
        if a is not None:
            self.wait(eng, a, a.n)
            if sem is None:
                sem = a
        if sem is None:
            self.emit(eng, lambda e, _f=fn: _f(e))
            return None
        sem.n += k
        tgt = sem.n

        def f(e, _f=fn, _h=sem.h, _k=k):
            _f(e).then_inc(_h, _k)

        self.emit(eng, f)
        return tgt


def build_program():
    nc = bass.Bass("TRN2", target_bir_lowering=False)

    X = nc.dram_tensor("X", [TB, D], F32, kind="ExternalInput").ap()
    Wq = nc.dram_tensor("Wq", [D, 4096], F32, kind="ExternalInput").ap()
    Wk = nc.dram_tensor("Wk", [D, D], F32, kind="ExternalInput").ap()
    Wv = nc.dram_tensor("Wv", [D, D], F32, kind="ExternalInput").ap()
    Wo = nc.dram_tensor("Wo", [4096, D], F32, kind="ExternalInput").ap()
    W1 = nc.dram_tensor("W1", [HFF, D], F32, kind="ExternalInput").ap()
    W2 = nc.dram_tensor("W2", [HFF, D], F32, kind="ExternalInput").ap()
    W3 = nc.dram_tensor("W3", [D, HFF], F32, kind="ExternalInput").ap()
    COSB = nc.dram_tensor("COSB", [256, TB], F32, kind="ExternalInput").ap()
    SINB = nc.dram_tensor("SINB", [256, TB], F32, kind="ExternalInput").ap()
    MSK = nc.dram_tensor("MSK", [16 * 128, 512], BF16, kind="ExternalInput").ap()
    Y = nc.dram_tensor("Y", [OWN, D], F32, kind="ExternalOutput").ap()

    xn1s = nc.dram_tensor("xn1s", [TB, D], BF16, kind="Internal").ap()
    Ks = nc.dram_tensor("Ks", [D, TB], BF16, kind="Internal").ap()
    Vs2 = nc.dram_tensor("Vs2", [4 * TB, 512], BF16, kind="Internal").ap()
    x2s = nc.dram_tensor("x2s", [OWN, D], F32, kind="Internal").ap()
    xn2s = nc.dram_tensor("xn2s", [OWN, D], BF16, kind="Internal").ap()
    w1qn = nc.dram_tensor("w1qn", [HFF, D], BF16, kind="Internal").ap()
    w2qn = nc.dram_tensor("w2qn", [HFF, D], BF16, kind="Internal").ap()
    w3qn = nc.dram_tensor("w3qn", [D, HFF], BF16, kind="Internal").ap()
    w1qt = nc.dram_tensor("w1qt", [D, HFF], BF16, kind="Internal").ap()
    w2qt = nc.dram_tensor("w2qt", [D, HFF], BF16, kind="Internal").ap()

    def r3(t, p=128):
        return t.rearrange("(c p) f -> p c f", p=p)

    with contextlib.ExitStack() as stk:
        E = stk.enter_context
        R1 = E(nc.sbuf_tensor("R1", [128, 32768], BF16))
        R2 = E(nc.sbuf_tensor("R2", [128, 16384], BF16))
        R3 = E(nc.sbuf_tensor("R3", [128, 16384], BF16))
        R4 = E(nc.sbuf_tensor("R4", [128, 16384], BF16))
        qp = E(nc.sbuf_tensor("qp", [128, 4096], BF16))
        pbuf = E(nc.sbuf_tensor("pbuf", [128, 2048], BF16))
        tmp = E(nc.sbuf_tensor("tmp", [128, 2048], F32))
        bcs = E(nc.sbuf_tensor("bcs", [128, 1024], F32))
        onec = E(nc.sbuf_tensor("onec", [128, 1], BF16))
        onecf = E(nc.sbuf_tensor("onecf", [128, 1], F32))
        onerf = E(nc.sbuf_tensor("onerf", [1, 128], F32))
        sml = E(nc.sbuf_tensor("sml", [128, 16], F32))
        ln2m = E(nc.sbuf_tensor("ln2m", [128, 12], F32))
        gpart = E(nc.sbuf_tensor("gpart", [128, 192], F32))
        gsml = E(nc.sbuf_tensor("gsml", [1, 8], F32))
        thrP = E(nc.sbuf_tensor("thrP", [128, 6], F32))
        xres = E(nc.sbuf_tensor("xres", [128, 2048], F32))
        sgt = E(nc.sbuf_tensor("sgt", [128, 2048], BF16))
        w3rt = E(nc.sbuf_tensor("w3rt", [128, 1024], BF16))
        tch = E(nc.sbuf_tensor("tch", [128, 2816], BF16))

        PS = [E(nc.psum_tensor(f"PS{i}", [128, 1024], F32)) for i in range(4)]

        names = (["TE", "VE", "AE", "PE",
                  "XL0", "XL1", "XS0", "XS1", "XF", "CSB", "CSA",
                  "KH0", "KH1", "VH0", "VH1", "KST0", "KST1", "VST0", "VST1",
                  "X2S", "XN2", "CH", "CH1", "USE0", "USE1", "W3R0", "W3R1", "OUT",
                  "WK0", "WK1", "WV0", "WV1", "WQP", "WO0", "WO1",
                  "GL0", "GL1", "QL0", "QL1", "QS0", "QS1"]
                 + [f"XR{i}" for i in range(8)])
        SM = {nm: Sem(E(nc.semaphore(nm))) for nm in names}
        TE, VE, AE, PE = SM["TE"], SM["VE"], SM["AE"], SM["PE"]
        XL = [SM["XL0"], SM["XL1"]]
        XS = [SM["XS0"], SM["XS1"]]
        XF = SM["XF"]
        CSB, CSA = SM["CSB"], SM["CSA"]
        KH = [SM["KH0"], SM["KH1"]]
        VH = [SM["VH0"], SM["VH1"]]
        KST = [SM["KST0"], SM["KST1"]]
        VST = [SM["VST0"], SM["VST1"]]
        X2S, XN2 = SM["X2S"], SM["XN2"]
        CHs = [SM["CH"], SM["CH1"]]
        USE = [SM["USE0"], SM["USE1"]]
        W3R = [SM["W3R0"], SM["W3R1"]]
        OUT = SM["OUT"]
        WKs = [SM["WK0"], SM["WK1"]]
        WVs = [SM["WV0"], SM["WV1"]]
        WQP = SM["WQP"]
        WOs = [SM["WO0"], SM["WO1"]]
        GL = [SM["GL0"], SM["GL1"]]
        QL = [SM["QL0"], SM["QL1"]]
        QS = [SM["QS0"], SM["QS1"]]
        XR = [SM[f"XR{i}"] for i in range(8)]

        block = E(nc.Block())

        b = Build()
        b.auto = {"vector": VE, "scalar": AE, "gpsimd": PE}
        R1f = R1[:].bitcast(F32)
        R2f = R2[:].bitcast(F32)
        R3f = R3[:].bitcast(F32)
        R4f = R4[:].bitcast(F32)

        def ps(i, j):
            return PS[i][:, j * 512:(j + 1) * 512]

        bg = []
        bg_i = [0]

        def pump(n):
            while n > 0 and bg_i[0] < len(bg):
                bg[bg_i[0]]()
                bg_i[0] += 1
                n -= 1

        def pump_all():
            pump(len(bg) - bg_i[0])

        ntile_w = [44, 44, 64]
        width_w = [2048, 2048, 1408]
        WMAT = [W1, W2, W3]
        WQN = [w1qn, w2qn, w3qn]

        def wsrc(mi, g):
            if mi < 2:
                return WMAT[mi][g * 128:(g + 1) * 128, :]
            rt, cc = divmod(g, 4)
            return W3[rt * 128:(rt + 1) * 128, cc * 1408:(cc + 1) * 1408]

        def wqdst(mi, g):
            if mi < 2:
                return WQN[mi][g * 128:(g + 1) * 128, :]
            rt, cc = divmod(g, 4)
            return w3qn[rt * 128:(rt + 1) * 128, cc * 1408:(cc + 1) * 1408]

        gq = {
            "gred": {}, "pool_red_done": [0, 0, 0],
            "thr_ae": [0, 0, 0],
            "qsub": {}, "qstore": {},
            "qstore_done": [None, None, None],
            "chain_w": [[0, 0, 0, 0], [0, 0, 0, 0]],
        }

        def mk_gamma_job(mi, g):
            def job():
                sl = g % 2
                wd = width_w[mi]
                gi = (0, 44, 88)[mi] + g
                prev = None
                if g >= 2:
                    prev = gq["gred"].get((mi, g - 2))
                elif mi > 0:
                    prev = gq["gred"].get((mi - 1, ntile_w[mi - 1] - 2 + g))
                if prev:
                    b.wait("gpsimd", AE, prev)
                t = b.dma("gpsimd", GL[sl], R4f[:, sl * 2048: sl * 2048 + wd],
                          wsrc(mi, g))
                b.wait("scalar", GL[sl], t)
                gq["gred"][(mi, g)] = b.op(
                    "scalar",
                    lambda e, _s=sl, _w=wd, _c=gi: e.activation(
                        R4[:, 8192 + (_s % 2) * 4096: 8192 + (_s % 2) * 4096 + _w],
                        R4f[:, _s * 2048: _s * 2048 + _w], AF.Abs,
                        accum_out=gpart[:, _c:_c + 1]),
                    sem=AE)
                if g == ntile_w[mi] - 1:
                    gq["pool_red_done"][mi] = AE.n
            return job

        def mk_quant_job(mi, g):
            def job():
                sl = g % 2
                wd = width_w[mi]
                if g >= 2:
                    b.wait("gpsimd", VE, gq["qsub"][(mi, g - 2)])
                    b.wait("gpsimd", QS[sl], gq["qstore"][(mi, g - 2)])
                elif mi > 0:
                    pg = ntile_w[mi - 1] - 2 + g
                    b.wait("gpsimd", VE, gq["qsub"][(mi - 1, pg)])
                    b.wait("gpsimd", QS[sl], gq["qstore"][(mi - 1, pg)])
                else:
                    b.wait("gpsimd", AE, gq["pool_red_done"][2])
                t = b.dma("gpsimd", QL[sl], R4f[:, sl * 2048: sl * 2048 + wd],
                          wsrc(mi, g))
                a_ap = R4[:, 8192 + sl * 4096: 8192 + sl * 4096 + wd]
                b_ap = R4[:, 10240 + sl * 4096: 10240 + sl * 4096 + wd]
                o_ap = R4[:, sl * 4096: sl * 4096 + wd]
                w_ap = R4f[:, sl * 2048: sl * 2048 + wd]
                b.wait("vector", QL[sl], t)
                b.wait("vector", AE, gq["thr_ae"][mi])
                b.op("vector", lambda e, _w=w_ap, _a=a_ap, _mi=mi:
                     e.tensor_single_scalar(
                         _a, _w, thrP[:, 2 * _mi:2 * _mi + 1], ALU.is_gt))
                b.op("vector", lambda e, _w=w_ap, _b=b_ap, _mi=mi:
                     e.tensor_single_scalar(
                         _b, _w, thrP[:, 2 * _mi + 1:2 * _mi + 2],
                         ALU.is_lt))
                gq["qsub"][(mi, g)] = b.op(
                    "vector", lambda e, _a=a_ap, _b=b_ap, _o=o_ap:
                    e.tensor_sub(_o, _a, _b), sem=VE)
                b.wait("gpsimd", VE, gq["qsub"][(mi, g)])
                gq["qstore"][(mi, g)] = b.dma("gpsimd", QS[sl],
                                              wqdst(mi, g), o_ap)
                if g == ntile_w[mi] - 1:
                    gq["qstore_done"][mi] = (QS[0].n, QS[1].n)
            return job

        def mk_chain_job(w, hh, d):
            def job():
                sl = d % 2
                base = sl * 1408
                wqn_t = (w1qn, w2qn)[w]
                wqt_t = (w1qt, w2qt)[w]
                if hh == 0 and d == 0:
                    q0_, q1_ = gq["qstore_done"][w]
                    b.wait("sync", QS[0], q0_)
                    b.wait("sync", QS[1], q1_)
                b.wait("sync", CHs[sl], CHs[sl].n)
                t = b.dma("sync", CHs[sl], tch[:, base:base + 1408],
                          wqn_t[hh * 1408:(hh + 1) * 1408,
                                d * 128:(d + 1) * 128],
                          transpose=True)
                b.wait("sync", CHs[sl], t)
                b.dma("sync", CHs[sl],
                      wqt_t[d * 128:(d + 1) * 128,
                            hh * 1408:(hh + 1) * 1408],
                      tch[:, base:base + 1408])
                if d == 15:
                    gq["chain_w"][w][hh] = (CHs[0].n, CHs[1].n)
            return job

        for mi in range(3):
            for g in range(ntile_w[mi]):
                bg.append(mk_gamma_job(mi, g))
        for g in range(ntile_w[0]):
            bg.append(mk_quant_job(0, g))
        for hh in range(4):
            for d in range(16):
                bg.append(mk_chain_job(0, hh, d))
        for g in range(ntile_w[1]):
            bg.append(mk_quant_job(1, g))
        for hh in range(4):
            for d in range(16):
                bg.append(mk_chain_job(1, hh, d))
        for g in range(ntile_w[2]):
            bg.append(mk_quant_job(2, g))

        b.op("gpsimd", lambda e: e.memset(onec[:], 1.0))
        b.op("gpsimd", lambda e: e.memset(onecf[:], 1.0))
        b.op("gpsimd", lambda e: e.memset(onerf[:], 1.0))
        b.op("gpsimd", lambda e: e.memset(gpart[:], 0.0), sem=PE)
        for i in range(2):
            b.dma("sync", CSB, R2f[:, i * 2048:(i + 1) * 2048],
                  COSB[i * 128:(i + 1) * 128, :])
        for i in range(2):
            b.dma("sync", CSB, R2f[:, 4096 + i * 2048: 4096 + (i + 1) * 2048],
                  SINB[i * 128:(i + 1) * 128, :])
        csb_total = CSB.n

        xt_rel = [0, 0]
        st_store = [0] * 16
        for i in range(16):
            sl = i % 2
            p = sl * 8
            if i >= 2:
                b.wait("sync", VE, xt_rel[sl])
            xl = b.dma("sync", XL[sl], R3f[:, sl * 2048:(sl + 1) * 2048],
                       X[i * 128:(i + 1) * 128, :])
            xt = R3f[:, sl * 2048:(sl + 1) * 2048]
            stg = R1[:, sl * 2048:(sl + 1) * 2048]
            b.wait("vector", XL[sl], xl)
            if i >= 2:
                b.wait("vector", XS[sl], st_store[i - 2])
            b.op("vector", lambda e, _x=xt, _p=p: e.reduce_sum(
                sml[:, _p:_p + 1], _x, axis=AX.X))
            b.op("vector", lambda e, _x=xt, _p=p, _st=stg:
                 e.scalar_tensor_tensor(_st, _x, 1.0, _x, ALU.mult, ALU.mult,
                                        accum_out=sml[:, _p + 1:_p + 2]))
            b.op("vector", lambda e, _p=p: e.tensor_scalar(
                sml[:, _p + 2:_p + 3], sml[:, _p:_p + 1], 1.0 / D, 0.0,
                ALU.mult, ALU.add))
            b.op("vector", lambda e, _p=p: e.tensor_tensor(
                sml[:, _p + 3:_p + 4], sml[:, _p + 2:_p + 3],
                sml[:, _p + 2:_p + 3], ALU.mult))
            b.op("vector", lambda e, _p=p: e.scalar_tensor_tensor(
                sml[:, _p + 4:_p + 5], sml[:, _p + 1:_p + 2], 1.0 / D,
                sml[:, _p + 3:_p + 4], ALU.mult, ALU.subtract))
            vvar = b.op("vector", lambda e, _p=p: e.tensor_scalar(
                sml[:, _p + 4:_p + 5], sml[:, _p + 4:_p + 5], 1.0, 1e-5,
                ALU.mult, ALU.add), sem=VE)
            b.wait("scalar", VE, vvar)
            b.op("scalar", lambda e, _p=p: e.activation(
                sml[:, _p + 5:_p + 6], sml[:, _p + 4:_p + 5], AF.Ln))
            arstd = b.op("scalar", lambda e, _p=p: e.activation(
                sml[:, _p + 6:_p + 7], sml[:, _p + 5:_p + 6], AF.Exp,
                scale=-0.5), sem=AE)
            b.wait("vector", AE, arstd)
            vapp = b.op("vector", lambda e, _x=xt, _p=p, _st=stg:
                        e.tensor_scalar(_st, _x, sml[:, _p + 2:_p + 3],
                                        sml[:, _p + 6:_p + 7],
                                        ALU.subtract, ALU.mult), sem=VE)
            xt_rel[sl] = vapp
            b.wait("sync", VE, vapp)
            st_store[i] = b.dma("sync", XS[sl],
                                xn1s[i * 128:(i + 1) * 128, :], stg)
            pump(2)
        xs_totals = (XS[0].n, XS[1].n)

        b.wait("sync", XS[0], xs_totals[0])
        b.wait("sync", XS[1], xs_totals[1])
        for d in range(16):
            b.dma("sync", XF, R1[:, d * 2048:(d + 1) * 2048],
                  xn1s[:, d * 128:(d + 1) * 128], transpose=True)
        xf_fm = XF.n

        def xn1fm(d, c0, w):
            return R1[:, d * 2048 + c0: d * 2048 + c0 + w]

        wkp_load = [0, 0]
        kgrp_mm = {}
        krope_ve = {}
        kst_tgt = {}
        b.wait("tensor", XF, xf_fm)
        b.wait("vector", CSB, csb_total)
        for j in range(8):
            jsl = j % 2
            if j == 0:
                b.wait("gpsimd", VE, max(xt_rel))
            if j >= 2:
                b.wait("gpsimd", TE, kgrp_mm[(j - 2, 1)])
            wkp_load[jsl] = b.dma(
                "gpsimd", WKs[jsl],
                R3[:, jsl * 4096: jsl * 4096 + 4096].rearrange(
                    "p (c f) -> p c f", f=256),
                r3(Wk)[:, :, 256 * j:256 * (j + 1)])
            for th in range(2):
                par = (2 * j + th) % 2
                c0ps, c1ps = PS[2 * par], PS[2 * par + 1]
                b.wait("tensor", WKs[jsl], wkp_load[jsl])
                prev = (j - 1, th)
                if prev in krope_ve:
                    b.wait("tensor", VE, krope_ve[prev])
                for kc in range(2):
                    for n in range(2):
                        for dd in range(16):
                            b.op("tensor", lambda e, _kc=kc, _n=n, _d=dd,
                                 _jsl=jsl, _th=th,
                                 _ps=(c0ps, c1ps)[kc]: e.matmul(
                                _ps[:, _n * 512:(_n + 1) * 512],
                                R3[:, _jsl * 4096 + _d * 256 + _kc * 128:
                                   _jsl * 4096 + _d * 256 + _kc * 128 + 128],
                                xn1fm(_d, _th * 1024 + _n * 512, 512),
                                start=(_d == 0), stop=(_d == 15)),
                                sem=(TE if (kc == 1 and n == 1 and dd == 15)
                                     else None))
                kgrp_mm[(j, th)] = TE.n
                ksl = (2 * j + th) % 2
                kst = R3[:, 8192 + ksl * 2048: 8192 + (ksl + 1) * 2048]
                b.wait("vector", TE, kgrp_mm[(j, th)])
                prevk = (j - 1, th)
                if prevk in kst_tgt:
                    b.wait("vector", KST[ksl], kst_tgt[prevk])
                cA = R2f[:, th * 1024: th * 1024 + 1024]
                cB = R2f[:, 2048 + th * 1024: 2048 + th * 1024 + 1024]
                sA = R2f[:, 4096 + th * 1024: 4096 + th * 1024 + 1024]
                sB = R2f[:, 6144 + th * 1024: 6144 + th * 1024 + 1024]
                b.op("vector", lambda e, _c=c0ps, _cA=cA: e.tensor_tensor(
                    tmp[:, 0:1024], _c[:, 0:1024], _cA, ALU.mult))
                b.op("vector", lambda e, _c=c1ps, _sA=sA: e.tensor_tensor(
                    tmp[:, 1024:2048], _c[:, 0:1024], _sA, ALU.mult))
                b.op("vector", lambda e, _k=kst: e.tensor_sub(
                    _k[:, 0:1024], tmp[:, 0:1024], tmp[:, 1024:2048]))
                b.op("vector", lambda e, _c=c1ps, _cB=cB: e.tensor_tensor(
                    tmp[:, 0:1024], _c[:, 0:1024], _cB, ALU.mult))
                b.op("vector", lambda e, _c=c0ps, _sB=sB: e.tensor_tensor(
                    tmp[:, 1024:2048], _c[:, 0:1024], _sB, ALU.mult))
                krope_ve[(j, th)] = b.op("vector", lambda e, _k=kst:
                                         e.tensor_add(_k[:, 1024:2048],
                                                      tmp[:, 0:1024],
                                                      tmp[:, 1024:2048]),
                                         sem=VE)
                b.wait("sync", VE, krope_ve[(j, th)])
                t = 0
                for kc in range(2):
                    t = b.dma("sync", KST[ksl],
                              Ks[(2 * j + kc) * 128:(2 * j + kc + 1) * 128,
                                 th * 1024:(th + 1) * 1024],
                              kst[:, kc * 1024:(kc + 1) * 1024])
                kst_tgt[(j, th)] = t
                pump(4)
        kst_totals = (KST[0].n, KST[1].n)
        ve_kropes = VE.n

        def emit_thr(mi):
            b.wait("tensor", AE, gq["pool_red_done"][mi])
            if mi >= 1:
                b.wait("tensor", AE, gq["thr_ae"][mi - 1])
            base = (0, 44, 88)[mi]
            wdt = (44, 44, 64)[mi]
            tcr = b.op("tensor", lambda e, _b=base, _w=wdt: e.matmul(
                PS[3][0:1, 512:512 + _w], onecf[:, 0:1],
                gpart[:, _b:_b + _w], start=True, stop=True), sem=TE)
            b.wait("vector", TE, tcr)
            b.op("vector", lambda e, _mi=mi, _w=wdt: e.reduce_sum(
                gsml[0:1, _mi:_mi + 1], PS[3][0:1, 512:512 + _w],
                axis=AX.X))
            b.op("vector", lambda e, _mi=mi: e.tensor_scalar(
                gsml[0:1, 3 + _mi:4 + _mi], gsml[0:1, _mi:_mi + 1],
                0.5 / NW, 0.5e-8, ALU.mult, ALU.add))
            vthr = b.op("vector", lambda e, _mi=mi: e.tensor_scalar(
                gsml[0:1, 7:8], gsml[0:1, 3 + _mi:4 + _mi], -1.0, 0.0,
                ALU.mult, ALU.add), sem=VE)
            b.wait("tensor", VE, vthr)
            b.op("tensor", lambda e, _mi=mi: e.matmul(
                PS[3][:, 512:513], onerf[0:1, :], gsml[0:1, 3 + _mi:4 + _mi],
                start=True, stop=True))
            tbc = b.op("tensor", lambda e: e.matmul(
                PS[3][:, 513:514], onerf[0:1, :], gsml[0:1, 7:8],
                start=True, stop=True), sem=TE)
            b.wait("scalar", TE, tbc)
            gq["thr_ae"][mi] = b.op("scalar", lambda e, _mi=mi: e.activation(
                thrP[:, 2 * _mi:2 * _mi + 2], PS[3][:, 512:514], AF.Copy),
                sem=AE)

        b.wait("tensor", VE, ve_kropes)
        pump(max(0, 88 - bg_i[0]))
        emit_thr(0)
        emit_thr(1)

        wvq_load = [0, 0]
        v_mm = {}
        v_evac = {}
        vst_tgt = {}
        for fq in range(4):
            fsl = fq % 2
            if fq == 0:
                b.wait("gpsimd", TE, kgrp_mm[(7, 1)])
            if fq == 1:
                b.wait("gpsimd", KST[0], kst_totals[0])
                b.wait("gpsimd", KST[1], kst_totals[1])
            if fq >= 2:
                b.wait("gpsimd", TE, v_mm[(fq - 2, 15)])
            wvq_load[fsl] = b.dma(
                "gpsimd", WVs[fsl],
                R3[:, fsl * 8192: fsl * 8192 + 8192].rearrange(
                    "p (c f) -> p c f", f=512),
                r3(Wv)[:, :, 512 * fq:512 * (fq + 1)])
            b.wait("tensor", WVs[fsl], wvq_load[fsl])
            for tt in range(16):
                vpar = tt % 2
                prev = (fq, tt - 2) if tt >= 2 else \
                    ((fq - 1, 14 + tt) if fq else None)
                if prev in v_evac:
                    b.wait("tensor", AE, v_evac[prev])
                for dd in range(16):
                    b.op("tensor", lambda e, _d=dd, _tt=tt, _fsl=fsl,
                         _vp=vpar: e.matmul(
                        ps(0, _vp),
                        xn1fm(_d, _tt * 128, 128),
                        R3[:, _fsl * 8192 + _d * 512:
                           _fsl * 8192 + (_d + 1) * 512],
                        start=(_d == 0), stop=(_d == 15)),
                        sem=(TE if dd == 15 else None))
                v_mm[(fq, tt)] = TE.n
                b.wait("scalar", TE, v_mm[(fq, tt)])
                gi = fq * 4 + (tt // 4)
                if tt % 4 == 0:
                    pv = vst_tgt.get(gi - 2)
                    if pv:
                        b.wait("scalar", VST[gi % 2], pv)
                gh = (gi % 2) * 2048
                v_evac[(fq, tt)] = b.op(
                    "scalar", lambda e, _tt=tt, _vp=vpar, _gh=gh:
                    e.activation(
                        qp[:, _gh + (_tt % 4) * 512:
                           _gh + ((_tt % 4) + 1) * 512],
                        ps(0, _vp), AF.Copy), sem=AE)
                if tt % 4 == 3:
                    grp = tt // 4
                    b.wait("sync", AE, v_evac[(fq, tt)])
                    vst_tgt[gi] = b.dma(
                        "sync", VST[gi % 2],
                        Vs2[fq * 2048 + grp * 512:
                            fq * 2048 + (grp + 1) * 512,
                            :].rearrange("(t p) f -> p t f", p=128),
                        qp[:, gh:gh + 2048].rearrange(
                            "p (t f) -> p t f", f=512))
                pump(2)
        vst_totals = (VST[0].n, VST[1].n)
        te_vdone = TE.n
        pump(max(0, 152 - bg_i[0]))
        b.wait("tensor", AE, gq["thr_ae"][1])
        emit_thr(2)

        b.wait("sync", VE, ve_kropes)
        b.dma("sync", CSA, R2[:, 0:8192].rearrange("p (m f) -> p m f", f=512),
              MSK.rearrange("(m p) f -> p m f", p=128))
        for i, (srct, off) in enumerate([(COSB, 4096), (COSB, 5120),
                                         (SINB, 6144), (SINB, 7168)]):
            half = i % 2
            b.dma("sync", CSA, R2f[:, off:off + 1024],
                  srct[half * 128:(half + 1) * 128, 0:1024])
        csa_total = CSA.n

        def y_blk(h, qhl, vc):
            blk = 4 * h + 2 * qhl + vc
            return R1[:, blk * 512:(blk + 1) * 512]

        kh_load = {}
        vh_load = {}
        wqp_load = {}
        qbuild_te = {}
        qrope_ve = {}
        ev_pv = {}
        ev_tanh = {}
        ev_exp = {}
        ev_mask = {}
        ynorm_ve = {}
        wo_stop = {}
        wo_evac = {}
        x2store = {}
        xn2store = {}
        first_vec_att = [True]
        last_ynorm = [0]
        ve_after_ln2 = [0, 0]
        qbuild_pair_te = {}

        for s in range(2):
            EV = EA if s == 0 else EB
            nev = len(EV)
            if s == 0:
                b.wait("sync", TE, te_vdone)
            if s == 1:
                b.wait("sync", TE, qbuild_te[(0, 7)])
                b.wait("sync", X2S, x2store[0])
                b.wait("sync", VE, ve_after_ln2[0])
            for d in range(16):
                b.dma("sync", XF,
                      R1[:, 16384 + d * 512: 16384 + (d + 1) * 512],
                      xn1s[s * 512:(s + 1) * 512, d * 128:(d + 1) * 128],
                      transpose=True)
            xf_sl = XF.n

            def emit_loads(h):
                hsl = h % 2
                if (s, h) == (0, 0):
                    b.wait("sync", KST[0], kst_totals[0])
                    b.wait("sync", KST[1], kst_totals[1])
                    b.wait("sync", VST[0], vst_totals[0])
                    b.wait("sync", VST[1], vst_totals[1])
                rel = ev_pv.get((s, h - 2, nev - 1)) or \
                    (ev_pv.get((0, 6 + h, 7)) if s == 1 else None)
                if rel:
                    b.wait("sync", TE, rel)
                if (s, h) in [(0, 0), (0, 1)]:
                    b.wait("sync", TE, te_vdone)
                if (s, h) in [(1, 0), (1, 1)]:
                    b.wait("sync", TE, wo_stop[(0, 3)])
                kh_load[(s, h)] = b.dma(
                    "sync", KH[hsl],
                    R3[:, 8192 + hsl * 4096:
                       8192 + (hsl + 1) * 4096].rearrange(
                        "p (c f) -> p c f", f=2048),
                    r3(Ks)[:, 2 * h:2 * h + 2, :])
                vh_load[(s, h)] = b.dma(
                    "sync", VH[hsl],
                    R1[:, 24576 + hsl * 4096:
                       24576 + (hsl + 1) * 4096].rearrange(
                        "p (c f) -> p c f", f=256),
                    r3(Vs2[(h // 2) * 2048:(h // 2 + 1) * 2048,
                           (h % 2) * 256:(h % 2) * 256 + 256]))
                relq = qbuild_te.get((s, h - 1)) or \
                    (wo_stop.get((0, 3)) if s == 1 and h == 0 else None)
                if relq:
                    b.wait("gpsimd", TE, relq)
                elif (s, h) == (0, 0):
                    b.wait("gpsimd", TE, te_vdone)
                wqp_load[(s, h)] = b.dma(
                    "gpsimd", WQP,
                    R3[:, 0:8192].rearrange("p (c f) -> p c f", f=512),
                    r3(Wq)[:, :, 512 * h:512 * (h + 1)])

            def emit_qbuild_pair(h, pr):
                hsl = h % 2
                b.wait("tensor", WQP, wqp_load[(s, h)])
                if pr == 0:
                    b.wait("tensor", XF, xf_sl)
                    if (s, h) == (0, 0):
                        b.wait("tensor", CSA, csa_total)
                    if h > 0:
                        b.wait("tensor", AE, ev_tanh[(s, h - 1, nev - 1)])
                    elif s == 1:
                        b.wait("tensor", AE, ev_tanh[(0, 7, 7)])
                else:
                    b.wait("tensor", VE, qrope_ve[(s, h, 0)])
                for qc in range(2):
                    for dd in range(16):
                        b.op("tensor", lambda e, _pr=pr, _qc=qc,
                             _d=dd: e.matmul(
                            ps(0, _qc),
                            R3[:, _d * 512 + (_pr * 2 + _qc) * 128:
                               _d * 512 + (_pr * 2 + _qc) * 128 + 128],
                            R1[:, 16384 + _d * 512:
                               16384 + (_d + 1) * 512],
                            start=(_d == 0), stop=(_d == 15)),
                            sem=(TE if (qc == 1 and dd == 15) else None))
                qbuild_pair_te[(s, h, pr)] = TE.n
                if pr == 1:
                    qbuild_te[(s, h)] = TE.n

            def emit_qrope(h, pr):
                hsl = h % 2
                if first_vec_att[0]:
                    b.wait("vector", CSA, csa_total)
                    b.wait("vector", VST[0], vst_totals[0])
                    b.wait("vector", VST[1], vst_totals[1])
                    first_vec_att[0] = False
                if h == 0 and pr == 0:
                    for _xi in range(4):
                        b.wait("vector", XR[_xi], XR[_xi].n)
                b.wait("vector", TE, qbuild_pair_te[(s, h, pr)])
                qo = hsl * 2048 + pr * 1024
                sfx = s * 512
                cA = R2f[:, 4096 + sfx: 4096 + sfx + 512]
                cB = R2f[:, 5120 + sfx: 5120 + sfx + 512]
                sA = R2f[:, 6144 + sfx: 6144 + sfx + 512]
                sB = R2f[:, 7168 + sfx: 7168 + sfx + 512]
                b.op("vector", lambda e, _cA=cA: e.tensor_tensor(
                    xres[:, 0:512], ps(0, 0), _cA, ALU.mult))
                b.op("vector", lambda e, _sA=sA: e.tensor_tensor(
                    xres[:, 512:1024], ps(0, 1), _sA, ALU.mult))
                b.op("vector", lambda e, _qo=qo: e.tensor_sub(
                    qp[:, _qo:_qo + 512], xres[:, 0:512],
                    xres[:, 512:1024]))
                b.op("vector", lambda e, _cB=cB: e.tensor_tensor(
                    xres[:, 0:512], ps(0, 1), _cB, ALU.mult))
                b.op("vector", lambda e, _sB=sB: e.tensor_tensor(
                    xres[:, 512:1024], ps(0, 0), _sB, ALU.mult))
                qrope_ve[(s, h, pr)] = b.op(
                    "vector", lambda e, _qo=qo: e.tensor_add(
                        qp[:, _qo + 512:_qo + 1024],
                        xres[:, 0:512], xres[:, 512:1024]), sem=VE)

            for h in range(8):
                hsl = h % 2
                if h == 0:
                    emit_loads(0)
                    emit_qbuild_pair(0, 0)
                    emit_qrope(0, 0)
                    emit_qbuild_pair(0, 1)
                    emit_qrope(0, 1)
                if h + 1 < 8:
                    emit_loads(h + 1)
                pump(2)

                b.wait("tensor", VE, qrope_ve[(s, h, 1)])
                b.wait("tensor", KH[hsl], kh_load[(s, h)])
                b.wait("tensor", VH[hsl], vh_load[(s, h)])
                qk_te = {}

                def emit_qk(ei, s=s, h=h, hsl=hsl, EV=EV, qk_te=qk_te):
                    stile = EV[ei]
                    if ei >= 1:
                        b.wait("tensor", AE, ev_tanh[(s, h, ei - 1)])
                    for kc in range(2):
                        for qhl in range(2):
                            b.op("tensor", lambda e, _kc=kc, _q=qhl,
                                 _st=stile, _hsl=hsl: e.matmul(
                                ps(0, _q),
                                R3[:, 8192 + _hsl * 4096 + _kc * 2048 +
                                   _st * 128: 8192 + _hsl * 4096 +
                                   _kc * 2048 + _st * 128 + 128],
                                qp[:, _hsl * 2048 + _q * 1024 + _kc * 512:
                                   _hsl * 2048 + _q * 1024 +
                                   (_kc + 1) * 512],
                                start=(_kc == 0), stop=(_kc == 1)),
                                sem=(TE if (kc == 1 and qhl == 1) else None))
                    qk_te[ei] = TE.n

                def emit_softcap(ei, s=s, h=h, nev=nev, qk_te=qk_te):
                    par = ei % 2
                    b.wait("scalar", TE, qk_te[ei])
                    b.op("scalar", lambda e, _par=par: e.activation(
                        tmp[:, _par * 1024: _par * 1024 + 512], ps(0, 0),
                        AF.Tanh, scale=0.00125))
                    ev_tanh[(s, h, ei)] = b.op(
                        "scalar", lambda e, _par=par: e.activation(
                            tmp[:, _par * 1024 + 512: _par * 1024 + 1024],
                            ps(0, 1), AF.Tanh, scale=0.00125), sem=AE)
                    if ei >= 2:
                        b.wait("scalar", TE, ev_pv[(s, h, ei - 2)])
                    elif (s, h) != (0, 0):
                        ph = (s, h - 1) if h else (0, 7)
                        pe_ = (7 if ph[0] == 0 else 15) - (1 - ei)
                        b.wait("scalar", TE, ev_pv[(ph[0], ph[1], pe_)])
                    if (s, h, ei) == (1, 0, 0):
                        b.wait("scalar", XN2, xn2store[(0, 3)])
                    b.op("scalar", lambda e, _par=par: e.activation(
                        pbuf[:, _par * 1024: _par * 1024 + 512],
                        tmp[:, _par * 1024: _par * 1024 + 512],
                        AF.Exp, scale=50.0))
                    ev_exp[(s, h, ei)] = b.op(
                        "scalar", lambda e, _par=par: e.activation(
                            pbuf[:, _par * 1024 + 512: _par * 1024 + 1024],
                            tmp[:, _par * 1024 + 512: _par * 1024 + 1024],
                            AF.Exp, scale=50.0), sem=AE)

                def emit_mask(ei, s=s, h=h, EV=EV):
                    stile = EV[ei]
                    par = ei % 2
                    if not ((s == 0) or (stile in MASKED_B)):
                        return
                    midx = ei if s == 0 else \
                        8 + [4, 5, 6, 7, 12, 13, 14, 15].index(stile)
                    b.wait("vector", AE, ev_exp[(s, h, ei)])
                    b.op("vector", lambda e, _par=par, _m=midx:
                         e.tensor_mul(
                             pbuf[:, _par * 1024: _par * 1024 + 512],
                             pbuf[:, _par * 1024: _par * 1024 + 512],
                             R2[:, _m * 512:(_m + 1) * 512]))
                    ev_mask[(s, h, ei)] = b.op(
                        "vector", lambda e, _par=par, _m=midx:
                        e.tensor_mul(
                            pbuf[:, _par * 1024 + 512: _par * 1024 + 1024],
                            pbuf[:, _par * 1024 + 512: _par * 1024 + 1024],
                            R2[:, _m * 512:(_m + 1) * 512]), sem=VE)

                def emit_pv(ei, s=s, h=h, hsl=hsl, EV=EV, nev=nev):
                    stile = EV[ei]
                    par = ei % 2
                    if (s == 0) or (stile in MASKED_B):
                        b.wait("tensor", VE, ev_mask[(s, h, ei)])
                    else:
                        b.wait("tensor", AE, ev_exp[(s, h, ei)])
                    if ei == 0 and (s, h) != (0, 0):
                        ph = (s, h - 1) if h else (0, 7)
                        b.wait("tensor", VE, ynorm_ve[ph])
                    first = (ei == 0)
                    lastE = (ei == nev - 1)
                    for vc in range(2):
                        for qhl in range(2):
                            b.op("tensor", lambda e, _vc=vc, _q=qhl,
                                 _st=stile, _hsl=hsl, _par=par, _f=first,
                                 _l=lastE: e.matmul(
                                ps(1 + _q, _vc),
                                R1[:, 24576 + _hsl * 4096 + _st * 256 +
                                   _vc * 128: 24576 + _hsl * 4096 +
                                   _st * 256 + _vc * 128 + 128],
                                pbuf[:, _par * 1024 + _q * 512:
                                     _par * 1024 + (_q + 1) * 512],
                                start=_f, stop=_l))
                    b.op("tensor", lambda e, _par=par, _f=first, _l=lastE:
                         e.matmul(PS[3][0:1, 0:512], onec[:, 0:1],
                                  pbuf[:, _par * 1024: _par * 1024 + 512],
                                  start=_f, stop=_l))
                    ev_pv[(s, h, ei)] = b.op(
                        "tensor", lambda e, _par=par, _f=first, _l=lastE:
                        e.matmul(PS[3][0:1, 512:1024], onec[:, 0:1],
                                 pbuf[:, _par * 1024 + 512:
                                      _par * 1024 + 1024],
                                 start=_f, stop=_l), sem=TE)

                emit_qk(0)
                emit_softcap(0)
                emit_mask(0)
                for ei in range(nev):
                    if ei + 1 < nev:
                        emit_qk(ei + 1)
                        emit_softcap(ei + 1)
                        emit_mask(ei + 1)
                    emit_pv(ei)
                    if ei % 2 == 1:
                        pump(1)

                b.wait("vector", TE, ev_pv[(s, h, nev - 1)])
                b.op("vector", lambda e: e.reciprocal(
                    tmp[0:1, 1024:1536], PS[3][0:1, 0:512]))
                vr2 = b.op("vector", lambda e: e.reciprocal(
                    tmp[0:1, 1536:2048], PS[3][0:1, 512:1024]), sem=VE)
                if h + 1 < 8:
                    emit_qbuild_pair(h + 1, 0)
                    emit_qrope(h + 1, 0)
                b.wait("tensor", VE, vr2)
                b.op("tensor", lambda e: e.matmul(
                    PS[3][:, 512:1024], onerf[0:1, :], tmp[0:1, 1024:1536],
                    start=True, stop=True))
                tb1 = b.op("tensor", lambda e: e.matmul(
                    PS[3][:, 0:512], onerf[0:1, :], tmp[0:1, 1536:2048],
                    start=True, stop=True), sem=TE)
                b.wait("scalar", TE, tb1)
                b.wait("scalar", VE, last_ynorm[0])
                b.op("scalar", lambda e: e.activation(
                    bcs[:, 0:512], PS[3][:, 512:1024], AF.Copy))
                ta1 = b.op("scalar", lambda e: e.activation(
                    bcs[:, 512:1024], PS[3][:, 0:512], AF.Copy), sem=AE)
                if h + 1 < 8:
                    emit_qbuild_pair(h + 1, 1)
                    emit_qrope(h + 1, 1)
                b.wait("vector", AE, ta1)
                for qhl in range(2):
                    for vc in range(2):
                        b.op("vector", lambda e, _q=qhl, _vc=vc, _h=h:
                             e.tensor_mul(y_blk(_h, _q, _vc),
                                          ps(1 + _q, _vc),
                                          bcs[:, _q * 512:(_q + 1) * 512]),
                             sem=VE)
                ynorm_ve[(s, h)] = VE.n
                last_ynorm[0] = VE.n
                pump(2)

            wo_load = {}
            for nq in range(4):
                for hf in range(2):
                    if nq == 0 and hf == 0:
                        b.wait("gpsimd", TE, ev_pv[(s, 7, nev - 1)])
                    elif nq and hf == 0:
                        b.wait("gpsimd", TE, wo_stop[(s, nq - 1)])
                    wo_load[(nq, hf)] = b.dma(
                        "gpsimd", WOs[hf],
                        R3[:, hf * 8192: hf * 8192 + 8192].rearrange(
                            "p (c f) -> p c f", f=512),
                        r3(Wo[hf * 2048:(hf + 1) * 2048,
                              nq * 512:(nq + 1) * 512]))
                wpar = nq % 2
                b.wait("tensor", VE, ynorm_ve[(s, 7)])
                if nq >= 2:
                    for tt in range(4):
                        b.wait("tensor", VE, wo_evac[(s, nq - 2, tt)])
                for kc in range(32):
                    hf = kc // 16
                    if kc % 16 == 0:
                        b.wait("tensor", WOs[hf], wo_load[(nq, hf)])
                    for tt in range(4):
                        b.op("tensor", lambda e, _kc=kc, _tt=tt, _hf=hf,
                             _wp=wpar: e.matmul(
                            ps(2 * _wp + _tt // 2, _tt % 2),
                            R1[:, _kc * 512 + _tt * 128:
                               _kc * 512 + _tt * 128 + 128],
                            R3[:, _hf * 8192 + (_kc % 16) * 512:
                               _hf * 8192 + (_kc % 16) * 512 + 512],
                            start=(_kc == 0), stop=(_kc == 31)),
                            sem=(TE if (kc == 31 and tt == 3) else None))
                wo_stop[(s, nq)] = TE.n
                for tt in range(4):
                    if nq == 0 and tt == 0:
                        b.wait("sync", VE, qrope_ve[(s, 7, 1)])
                    pv = wo_evac.get((s, nq - 1, tt)) if nq else None
                    if pv:
                        b.wait("sync", VE, pv)
                    xr = b.dma("sync", XR[tt],
                               xres[:, tt * 512:(tt + 1) * 512],
                               X[s * 512 + tt * 128:
                                 s * 512 + (tt + 1) * 128,
                                 nq * 512:(nq + 1) * 512])
                    b.wait("vector", XR[tt], xr)
                    if s == 1 and nq == 0 and tt == 0:
                        b.wait("vector", X2S, x2store[0])
                    b.wait("vector", TE, wo_stop[(s, nq)])
                    wo_evac[(s, nq, tt)] = b.op(
                        "vector", lambda e, _tt=tt, _wp=wpar, _nq=nq:
                        e.tensor_add(
                            R1f[:, 8192 + _tt * 2048 + _nq * 512:
                                8192 + _tt * 2048 + (_nq + 1) * 512],
                            ps(2 * _wp + _tt // 2, _tt % 2),
                            xres[:, _tt * 512:(_tt + 1) * 512]),
                        sem=VE)
                pump(2)

            for tt in range(4):
                for nq in range(4):
                    b.wait("vector", VE, wo_evac[(s, nq, tt)])
                xv = R1f[:, 8192 + tt * 2048: 8192 + (tt + 1) * 2048]
                b.op("vector", lambda e, _x=xv, _tt=tt: e.reduce_sum(
                    ln2m[:, _tt:_tt + 1], _x, axis=AX.X))
                b.op("vector", lambda e, _x=xv, _tt=tt:
                     e.scalar_tensor_tensor(
                         tmp[:, 0:2048], _x, 1.0, _x, ALU.mult, ALU.mult,
                         accum_out=ln2m[:, 4 + _tt:5 + _tt]))
                b.op("vector", lambda e, _tt=tt: e.tensor_scalar(
                    ln2m[:, _tt:_tt + 1], ln2m[:, _tt:_tt + 1], 1.0 / D,
                    0.0, ALU.mult, ALU.add))
                b.op("vector", lambda e, _tt=tt: e.tensor_tensor(
                    ln2m[:, 8 + _tt:9 + _tt], ln2m[:, _tt:_tt + 1],
                    ln2m[:, _tt:_tt + 1], ALU.mult))
                b.op("vector", lambda e, _tt=tt: e.scalar_tensor_tensor(
                    ln2m[:, 4 + _tt:5 + _tt], ln2m[:, 4 + _tt:5 + _tt],
                    1.0 / D, ln2m[:, 8 + _tt:9 + _tt], ALU.mult,
                    ALU.subtract))
                v2 = b.op("vector", lambda e, _tt=tt: e.tensor_scalar(
                    ln2m[:, 4 + _tt:5 + _tt], ln2m[:, 4 + _tt:5 + _tt],
                    1.0, 1e-5, ALU.mult, ALU.add), sem=VE)
                b.wait("scalar", VE, v2)
                b.op("scalar", lambda e, _tt=tt: e.activation(
                    ln2m[:, 4 + _tt:5 + _tt], ln2m[:, 4 + _tt:5 + _tt],
                    AF.Ln))
                a2 = b.op("scalar", lambda e, _tt=tt: e.activation(
                    ln2m[:, 8 + _tt:9 + _tt], ln2m[:, 4 + _tt:5 + _tt],
                    AF.Exp, scale=-0.5), sem=AE)
                b.wait("vector", AE, a2)
                if (s, tt) != (0, 0):
                    prevst = xn2store.get((s, tt - 1)) or xn2store.get((0, 3))
                    if prevst:
                        b.wait("vector", XN2, prevst)
                ap2 = b.op("vector", lambda e, _x=xv, _tt=tt:
                           e.tensor_scalar(
                               pbuf[:, 0:2048], _x, ln2m[:, _tt:_tt + 1],
                               ln2m[:, 8 + _tt:9 + _tt], ALU.subtract,
                               ALU.mult), sem=VE)
                b.wait("sync", VE, ap2)
                xn2store[(s, tt)] = b.dma(
                    "sync", XN2,
                    xn2s[s * 512 + tt * 128: s * 512 + (tt + 1) * 128, :],
                    pbuf[:, 0:2048])
                pump(1)
            ve_after_ln2[s] = VE.n
            b.wait("sync", VE, wo_evac[(s, 3, 3)])
            x2store[s] = b.dma(
                "sync", X2S,
                x2s[s * 512:(s + 1) * 512, :].rearrange(
                    "(t p) f -> p t f", p=128),
                R1f[:, 8192:16384].rearrange("p (t f) -> p t f", f=2048))

        pump_all()
        b.wait("sync", VE, gq["qsub"][(2, ntile_w[2] - 1)])
        q0, q1 = gq["qstore_done"][2]
        b.wait("sync", QS[0], q0)
        b.wait("sync", QS[1], q1)
        b.wait("sync", XN2, xn2store[(1, 3)])
        for d in range(16):
            b.dma("sync", XF, R4[:, d * 1024:(d + 1) * 1024],
                  xn2s[:, d * 128:(d + 1) * 128], transpose=True)
        xf_xn2 = XF.n

        def hid(hc, c0, w):
            if hc < 32:
                return R1[:, hc * 1024 + c0: hc * 1024 + c0 + w]
            return R2[:, (hc - 32) * 1024 + c0: (hc - 32) * 1024 + c0 + w]

        use_tgt = {}
        use_cons = {}
        sig_ae = {}
        hmul_ve = {}
        for hc in range(44):
            par = hc % 2
            hh = min(hc // 11, 3)
            for m in range(2):
                c0_, c1_ = gq["chain_w"][m][hh]
                b.wait("sync", CHs[0], c0_)
                b.wait("sync", CHs[1], c1_)
                prev = use_cons.get(hc - 2)
                if prev:
                    b.wait("sync", TE, prev)
                wqt_t = (w1qt, w2qt)[m]
                use_tgt[(hc, m)] = b.dma(
                    "sync", USE[par],
                    R3[:, m * 4096 + par * 2048:
                       m * 4096 + (par + 1) * 2048].rearrange(
                        "p (c f) -> p c f", f=128),
                    r3(wqt_t)[:, :, hc * 128:(hc + 1) * 128])
            b.wait("tensor", USE[par], use_tgt[(hc, 1)])
            if hc == 0:
                b.wait("tensor", XF, xf_xn2)
            if hc >= 2:
                b.wait("tensor", AE, sig_ae[hc - 2])
                b.wait("tensor", VE, hmul_ve[hc - 2])
            for m in range(2):
                for n in range(2):
                    for dd in range(16):
                        b.op("tensor", lambda e, _m=m, _n=n, _d=dd,
                             _p=par: e.matmul(
                            ps(2 * _p + _m, _n),
                            R3[:, _m * 4096 + _p * 2048 + _d * 128:
                               _m * 4096 + _p * 2048 + (_d + 1) * 128],
                            R4[:, _d * 1024 + _n * 512:
                               _d * 1024 + (_n + 1) * 512],
                            start=(_d == 0), stop=(_d == 15)),
                            sem=(TE if (m == 1 and n == 1 and dd == 15)
                                 else None))
            xte = TE.n
            use_cons[hc] = xte
            b.wait("scalar", TE, xte)
            if hc >= 2:
                b.wait("scalar", VE, hmul_ve[hc - 2])
            for n in range(2):
                sig_ae[hc] = b.op("scalar", lambda e, _n=n, _p=par:
                                  e.activation(
                                      sgt[:, _p * 1024 + _n * 512:
                                          _p * 1024 + (_n + 1) * 512],
                                      ps(2 * _p, _n), AF.Sigmoid), sem=AE)
            b.wait("vector", AE, sig_ae[hc])
            if hc == 0:
                b.wait("vector", X2S, x2store[1])
            for n in range(2):
                hmul_ve[hc] = b.op("vector", lambda e, _n=n, _p=par, _hc=hc:
                                   e.tensor_mul(
                                       hid(_hc, _n * 512, 512),
                                       ps(2 * _p + 1, _n),
                                       sgt[:, _p * 1024 + _n * 512:
                                           _p * 1024 + (_n + 1) * 512]),
                                   sem=VE)
        ve_hidden = VE.n

        w3_tgt = {}
        w3_cons = {}
        w3_evac = {}
        out_store = {}
        for nq in range(4):
            for hc in range(44):
                w3par = hc % 2
                if (nq, hc) == (0, 0):
                    b.wait("sync", QS[0], q0)
                    b.wait("sync", QS[1], q1)
                prev = w3_cons.get((nq, hc - 2)) or \
                    (w3_cons.get((nq - 1, 42 + hc)) if nq else None)
                if prev:
                    b.wait("sync", TE, prev)
                w3_tgt[(nq, hc)] = b.dma(
                    "sync", W3R[w3par],
                    w3rt[:, w3par * 512:(w3par + 1) * 512],
                    w3qn[nq * 512:(nq + 1) * 512, hc * 128:(hc + 1) * 128],
                    transpose=True)
                b.wait("tensor", W3R[w3par], w3_tgt[(nq, hc)])
                if hc == 0:
                    b.wait("tensor", VE, ve_hidden)
                    if nq >= 1:
                        for tt in range(8):
                            b.wait("tensor", VE, w3_evac[(nq - 1, tt)])
                for tt in range(8):
                    b.op("tensor", lambda e, _tt=tt, _hc=hc, _p=w3par:
                         e.matmul(
                             ps(_tt // 2, _tt % 2),
                             hid(_hc, _tt * 128, 128),
                             w3rt[:, _p * 512:(_p + 1) * 512],
                             start=(_hc == 0), stop=(_hc == 43)),
                         sem=(TE if tt == 7 else None))
                w3_cons[(nq, hc)] = TE.n
            w3te = TE.n
            for tt in range(8):
                pv = w3_evac.get((nq - 1, tt)) if nq else None
                if pv:
                    b.wait("sync", VE, pv)
                if nq == 0 and tt == 0:
                    b.wait("sync", X2S, x2store[1])
                    b.wait("sync", VE, wo_evac[(1, 3, 3)])
                    for _xi in range(8):
                        b.wait("sync", XR[_xi], XR[_xi].n)
                if tt >= 4:
                    b.wait("sync", VE, w3_evac[(nq, tt - 4)])
                xr = b.dma("sync", XR[tt],
                           xres[:, (tt % 4) * 512:((tt % 4) + 1) * 512],
                           x2s[tt * 128:(tt + 1) * 128,
                               nq * 512:(nq + 1) * 512])
                b.wait("vector", XR[tt], xr)
                b.wait("vector", TE, w3te)
                if nq and tt == 0:
                    b.wait("vector", OUT, out_store[nq - 1])
                w3_evac[(nq, tt)] = b.op(
                    "vector", lambda e, _tt=tt: e.tensor_add(
                        R3f[:, 4096 + _tt * 512: 4096 + (_tt + 1) * 512],
                        ps(_tt // 2, _tt % 2),
                        xres[:, (_tt % 4) * 512:((_tt % 4) + 1) * 512]),
                    sem=VE)
            b.wait("sync", VE, w3_evac[(nq, 7)])
            out_store[nq] = b.dma(
                "sync", OUT,
                Y.rearrange("(t p) f -> p t f",
                            p=128)[:, :, nq * 512:(nq + 1) * 512],
                R3f[:, 4096:8192].rearrange("p (t f) -> p t f", f=512))
        b.wait("sync", OUT, out_store[3])

        assert bg_i[0] == len(bg), f"unpumped bg jobs: {bg_i[0]}/{len(bg)}"

        @block.tensor
        def _(eng):
            for f in b.q["tensor"]:
                f(eng)

        @block.vector
        def _(eng):
            for f in b.q["vector"]:
                f(eng)

        @block.scalar
        def _(eng):
            for f in b.q["scalar"]:
                f(eng)

        @block.gpsimd
        def _(eng):
            for f in b.q["gpsimd"]:
                f(eng)

        @block.sync
        def _(eng):
            for f in b.q["sync"]:
                f(eng)

    return nc


_NC_CACHE = [None]


def _host_tables():
    inv = 1.0 / (10000.0 ** (np.arange(0, 256, 2, dtype=np.float64) / 256.0))
    return inv


def kernel(X=None, **kw):
    if X is None:
        X = kw.pop("L0_attn_X")
    Wq = kw.pop("L0_attn_Wq")
    Wk = kw.pop("L0_attn_Wk")
    Wv = kw.pop("L0_attn_Wv")
    Wo = kw.pop("L0_attn_Wo")
    W1 = kw.pop("L0_ffn_w1")
    W2 = kw.pop("L0_ffn_w2")
    W3 = kw.pop("L0_ffn_w3")
    X = np.asarray(X, dtype=np.float32)
    B = X.shape[0]

    inv = _host_tables()
    pos_all = np.arange(TB, dtype=np.float64)

    tri = np.zeros((4, 128, 512), dtype=np.float32)
    for j in range(4):
        tri[j] = (np.arange(512)[None, :] >=
                  (np.arange(128)[:, None] + 128 * j))
    ones_t = np.ones((128, 512), np.float32)
    zeros_t = np.zeros((128, 512), np.float32)

    if _NC_CACHE[0] is None:
        _NC_CACHE[0] = build_program()
    nc = _NC_CACHE[0]

    in_maps = []
    metas = []
    for core in range(8):
        bb, sg_ = divmod(core, 2)
        order = [0, 3, 1, 2] if sg_ == 0 else [1, 2, 0, 3]
        rows = np.concatenate([np.arange(c * 512, (c + 1) * 512)
                               for c in order])
        Xp = np.ascontiguousarray(X[bb][rows])
        posg = pos_all[rows]
        ang = inv[:, None] * posg[None, :]
        cosb = np.concatenate([np.cos(ang), np.cos(ang)], 0).astype(np.float32)
        sinb = np.concatenate([np.sin(ang), np.sin(ang)], 0).astype(np.float32)
        mtiles = [tri[j] for j in range(4)]
        mtiles += [zeros_t if sg_ == 0 else ones_t] * 4
        mtiles += [tri[j] for j in range(4)]
        mtiles += [ones_t if sg_ == 0 else zeros_t] * 4
        msk = np.concatenate(mtiles, 0).astype(ml_dtypes.bfloat16)
        in_maps.append({
            "X": Xp, "Wq": np.asarray(Wq, np.float32),
            "Wk": np.asarray(Wk, np.float32),
            "Wv": np.asarray(Wv, np.float32),
            "Wo": np.asarray(Wo, np.float32),
            "W1": np.asarray(W1, np.float32),
            "W2": np.asarray(W2, np.float32),
            "W3": np.asarray(W3, np.float32),
            "COSB": cosb, "SINB": sinb, "MSK": msk,
        })
        metas.append((bb, order))

    res = run_bass_kernel_spmd(nc, in_maps, core_ids=list(range(8)))

    out = np.empty((B, TB, D), dtype=np.float32)
    for core in range(8):
        bb, order = metas[core]
        Yc = np.asarray(res.results[core]["Y"])
        cA, cB = order[0], order[1]
        out[bb, cA * 512:(cA + 1) * 512] = Yc[0:512]
        out[bb, cB * 512:(cB + 1) * 512] = Yc[512:1024]
    return out
